# version 01
# baseline (speedup 1.0000x reference)
import contextlib
import numpy as np
import ml_dtypes
import concourse.bass as bass
import concourse.mybir as mybir
from concourse.bass_utils import run_bass_kernel_spmd

F32 = mybir.dt.float32
BF16 = mybir.dt.bfloat16
U32 = mybir.dt.uint32
AF = mybir.ActivationFunctionType
ALU = mybir.AluOpType
AX = mybir.AxisListType
AP = bass.AP

B, T, D = 8, 2048, 1024
NT = T // 128
NC = D // 128
LN_EPS = 1e-5
ENGS = ("sync", "tensor", "vector", "scalar", "gpsimd")
KB = 1024


def cdiv(a, b):
    return (a + b - 1) // b


class Ctr:
    def __init__(self, handle, step):
        self.h, self.step, self.n = handle, step, 0

    def bump(self):
        self.n += self.step
        return self.n


class Sched:

    def __init__(self, nc, es):
        self.nc = nc
        self.ops = {e: [] for e in ENGS}
        self.pending = {e: [] for e in ENGS}
        self.es = es
        self.ctrs = []

    def ctr(self, name, step=1):
        h = self.es.enter_context(self.nc.semaphore(name))
        c = Ctr(h, step)
        self.ctrs.append(c)
        return c

    def op(self, eng, fn, wait=(), inc=None):
        waits = list(self.pending[eng])
        self.pending[eng] = []
        for w in wait:
            c, v = w if isinstance(w, tuple) else (w, w.n)
            if v > 0:
                waits.append((c, v))
        val = inc.bump() if inc is not None else None
        self.ops[eng].append((waits, fn, inc))
        return val

    def barrier(self, engs=ENGS):
        snap = [(c, c.n) for c in self.ctrs if c.n > 0]
        for e in engs:
            self.pending[e] = list(snap)

    def emit(self, block):
        for eng_name in ENGS:
            if self.ops[eng_name] or self.pending[eng_name]:
                self._install(block, eng_name)

    def _install(self, block, eng_name):
        ops = self.ops[eng_name]
        pend = self.pending[eng_name]

        def body(eng):
            last = {}
            for waits, fn, inc in ops:
                for c, v in waits:
                    if last.get(id(c), 0) < v:
                        eng.wait_ge(c.h, v)
                        last[id(c)] = v
                inst = fn(eng)
                if inc is not None:
                    assert inst is not None, "op with inc returned None"
                    inst.then_inc(inc.h, inc.step)
            for c, v in pend:
                if last.get(id(c), 0) < v:
                    eng.wait_ge(c.h, v)
                    last[id(c)] = v

        getattr(block, eng_name)(body)


class Ctx:
    def __init__(self, nc, es):
        self.nc = nc
        self.es = es
        self.guards = {}

    def sb(self, name, shape, dt):
        return self.es.enter_context(self.nc.sbuf_tensor(name, shape, dt))

    def sb_at(self, name, shape, dt, off_bytes):
        return self.nc.alloc_sbuf_tensor_at(name, shape, dt, offset=off_bytes)

    def ps(self, name, shape, dt=F32):
        return self.es.enter_context(self.nc.psum_tensor(name, shape, dt))

    def guard(self, psname):
        return self.guards.pop(psname, [])

    def set_guard(self, psname, ctr, val):
        self.guards[psname] = [(ctr, val)]


def mm_chain(S, ctx, psname, ps_ap, pairs, wait=(), keep_open=False):
    pe = S.C["pe"]
    n = len(pairs)
    v = None
    for i, (l_ap, r_ap) in enumerate(pairs):
        w = (list(wait) + ctx.guard(psname)) if i == 0 else ()
        z = (i == n - 1) and not keep_open
        v = S.op("tensor",
                 lambda t, l_ap=l_ap, r_ap=r_ap, f=(i == 0), z=z:
                 t.matmul(ps_ap, l_ap, r_ap, start=f, stop=z),
                 wait=w, inc=pe)
    return v


ENG_CTR = {"vector": "ev", "scalar": "act", "gpsimd": "pool"}


def chain(S, eng, steps, wait=(), ctr=None):
    c = ctr or S.C[ENG_CTR[eng]]
    last = None
    for i, fn in enumerate(steps):
        w = wait if i == 0 else [(c, last)]
        last = S.op(eng, fn, wait=w, inc=c)
    return last



def emit_const_gen(S, ctx):
    ev, pool = S.C["ev"], S.C["pool"]
    g1 = S.op("gpsimd", lambda g: g.iota(
        ctx.scr128[:], pattern=[[1, 128]], base=0, channel_multiplier=-1,
        allow_small_or_imprecise_dtypes=True), inc=pool)

    chain(S, "vector", [
        lambda v: v.tensor_scalar(ctx.identf[:], ctx.scr128[:], 0.0, None,
                                  ALU.is_equal),
        lambda v: v.tensor_copy(ctx.ident[:], ctx.identf[:]),
        lambda v: v.memset(ctx.onescol[:], 1.0),
        lambda v: v.memset(ctx.ones_bcol[:], 1.0),
    ], wait=[(pool, g1)])


def emit_ln(S, ctx, x_dram, xnT, xn_keep=None, stats_keep=None):
    C = S.C
    ld, pe, ev, act = C["ld"], C["pe"], C["ev"], C["act"]
    xt, xnb, st = ctx.ln_xt, ctx.ln_xnb, ctx.ln_st
    xr = x_dram.ap().rearrange("(n p) d -> n p d", p=128)
    war = [0, 0]
    xnb_war = [[], []]
    for i in range(NT):
        bi = i % 2
        lv = S.op("gpsimd", lambda g, i=i, bi=bi: g.dma_start(xt[:, bi, :], xr[i]),
                  wait=[(act, war[bi])], inc=ld)
        S.op("scalar", lambda sc, bi=bi: sc.activation(
            ctx.actdump[:, 0, :], xt[:, bi, :], AF.Identity,
            accum_out=st[:, 0:1]), wait=[(ld, lv)], inc=act)
        av = S.op("scalar", lambda sc, bi=bi: sc.activation(
            ctx.actdump[:, 1, :], xt[:, bi, :], AF.Square,
            accum_out=st[:, 1:2]), inc=act)

        e1 = chain(S, "vector", [
            lambda v: v.tensor_scalar(st[:, 0:1], st[:, 0:1], 1.0 / D, None,
                                      ALU.mult),
            lambda v: v.tensor_scalar(st[:, 1:2], st[:, 1:2], 1.0 / D, None,
                                      ALU.mult),
            lambda v: v.tensor_tensor(st[:, 2:3], st[:, 0:1], st[:, 0:1],
                                      ALU.mult),
            lambda v: v.tensor_tensor(st[:, 1:2], st[:, 1:2], st[:, 2:3],
                                      ALU.subtract),
            lambda v: v.tensor_scalar(st[:, 1:2], st[:, 1:2], LN_EPS, None,
                                      ALU.add),
        ], wait=[(act, av)])
        sv = S.op("scalar", lambda sc: sc.sqrt(st[:, 3:4], st[:, 1:2]),
                  wait=[(ev, e1)], inc=act)
        e2 = chain(S, "vector", [
            lambda v: v.reciprocal(st[:, 3:4], st[:, 3:4]),
            lambda v: v.tensor_tensor(st[:, 2:3], st[:, 0:1], st[:, 3:4],
                                      ALU.mult),
            lambda v: v.tensor_scalar(st[:, 2:3], st[:, 2:3], -1.0, None,
                                      ALU.mult),
        ], wait=[(act, sv)])
        xv = S.op("scalar", lambda sc, bi=bi: sc.activation(
            xnb[:, bi, :], xt[:, bi, :], AF.Identity,
            bias=st[:, 2:3], scale=st[:, 3:4]),
            wait=[(ev, e2)] + xnb_war[bi], inc=act)
        war[bi] = xv
        if stats_keep is not None:
            S.op("vector", lambda v, i=i: v.tensor_copy(
                stats_keep[:, i, 0:2], st[:, 2:4]), wait=[(ev, e2)], inc=ev)
        xnb_war[bi] = []
        if xn_keep is not None:
            kv = S.op("vector", lambda v, i=i, bi=bi: v.tensor_copy(
                xn_keep[:, i, :], xnb[:, bi, :]), wait=[(act, xv)], inc=ev)
            xnb_war[bi].append((ev, kv))
        if xnT is not None:
            for c in range(NC):
                pv = S.op("tensor", lambda t, bi=bi, c=c: t.transpose(
                    ctx.p_tb[:, 0:128], xnb[:, bi, 128 * c:128 * (c + 1)],
                    ctx.ident[:]), wait=[(act, xv)] + ctx.guard("p_tb"),
                    inc=pe)
                tv = S.op("vector", lambda v, i=i, c=c: v.tensor_copy(
                    xnT[:, c, 128 * i:128 * (i + 1)], ctx.p_tb[:, 0:128]),
                    wait=[(pe, pv)], inc=ev)
                ctx.set_guard("p_tb", ev, tv)
            xnb_war[bi].append((pe, pe.n))


def emit_proj_T(S, ctx, w_lhsT, xnT, outT, scale=1.0):
    C = S.C
    Dout = w_lhsT.shape[2]
    for dc in range(Dout // 128):
        for ft in range(T // 512):
            pairs = [(w_lhsT[:, cc, 128 * dc:128 * (dc + 1)],
                      xnT[:, cc, 512 * ft:512 * (ft + 1)]) for cc in range(NC)]
            v = mm_chain(S, ctx, "p_c", ctx.p_c[:, 0:512], pairs)
            e = S.op("scalar", lambda sc, dc=dc, ft=ft: sc.activation(
                outT[:, dc, 512 * ft:512 * (ft + 1)], ctx.p_c[:, 0:512],
                AF.Identity, scale=scale), wait=[(C["pe"], v)], inc=C["act"])
            ctx.set_guard("p_c", C["act"], e)


def emit_tr_f32(S, ctx, dst_ap, src_ap, k):
    C = S.C
    pv = S.op("tensor", lambda t: t.transpose(
        ctx.p_c[:, 0:k], src_ap, ctx.identf[:]),
        wait=ctx.guard("p_c"), inc=C["pe"])
    tv = S.op("vector", lambda v: v.tensor_copy(dst_ap, ctx.p_c[:, 0:k]),
              wait=[(C["pe"], pv)], inc=C["ev"])
    ctx.set_guard("p_c", C["ev"], tv)
    return tv


def emit_bcast_col(S, ctx, dst_col, src11, via_bf, wait=()):
    C = S.C
    ev, pe = C["ev"], C["pe"]
    cv = S.op("vector", lambda v: v.tensor_copy(via_bf, src11), wait=wait,
              inc=ev)
    pv = S.op("tensor", lambda t: t.matmul(
        ctx.p_c[:, 0:1], ctx.onescol[:], via_bf, start=True, stop=True),
        wait=[(ev, cv)] + ctx.guard("p_c"), inc=pe)
    tv = S.op("vector", lambda v: v.tensor_copy(dst_col, ctx.p_c[:, 0:1]),
              wait=[(pe, pv)], inc=ev)
    ctx.set_guard("p_c", ev, tv)
    return tv



def build_L0():
    nc = bass.Bass("TRN2", target_bir_lowering=False)
    TF = B * T

    def din(name, shape, dt=F32):
        return nc.dram_tensor(name, shape, dt, kind="ExternalInput")

    xb_d = din("xb", [T, D])
    xTf_d = din("xTf", [D, TF])
    wqk_d = din("wqk", [D, D])
    wvT_d = din("wvT", [D, D])
    slope_d = din("slope", [1, 1])
    wg_d = din("wg", [128, NC * 8])
    w1_d = din("w1", [8 * D, 4096])
    b1_d = din("b1", [8, 32 * 128])
    w2_d = din("w2", [8 * 4096, D])
    b2_d = din("b2", [8, D])
    sel_d = din("sel", [128, 16])
    x1_d = nc.dram_tensor("x1", [T, D], F32, kind="ExternalOutput")
    s_scr = nc.dram_tensor("s_scr", [T, D], BF16, kind="Internal")
    import os as _os
    DBG = _os.environ.get("KDBG", "") == "1"
    if DBG:
        dbg = {
            "d_xnT": nc.dram_tensor("d_xnT", [128, NC * T], BF16,
                                    kind="ExternalOutput"),
            "d_s": nc.dram_tensor("d_s", [T, D], BF16, kind="ExternalOutput"),
            "d_stats": nc.dram_tensor("d_stats", [128, 80], F32,
                                      kind="ExternalOutput"),
            "d_cmbc": nc.dram_tensor("d_cmbc", [128, 8 * 16], F32,
                                     kind="ExternalOutput"),
            "d_acc": nc.dram_tensor("d_acc", [128, NT * D], BF16,
                                    kind="ExternalOutput"),
            "d_ri": nc.dram_tensor("d_ri", [128, 2 * 128], F32,
                                   kind="ExternalOutput"),
            "d_rw": nc.dram_tensor("d_rw", [128, 2 * 128], F32,
                                   kind="ExternalOutput"),
            "d_qpT": nc.dram_tensor("d_qpT", [128, NC * T], BF16,
                                    kind="ExternalOutput"),
            "d_vtm": nc.dram_tensor("d_vtm", [128, NT * D], BF16,
                                    kind="ExternalOutput"),
        }

    es = contextlib.ExitStack()
    with es:
        ctx = Ctx(nc, es)
        S = Sched(nc, es)
        S.C = dict(ld=S.ctr("ld", 16), st=S.ctr("st", 16), pe=S.ctr("pe", 1),
                   ev=S.ctr("ev", 1), act=S.ctr("act", 1), pool=S.ctr("pool", 1))
        C = S.C
        ld, st_c, pe, ev, act, pool = (C["ld"], C["st"], C["pe"], C["ev"],
                                       C["act"], C["pool"])

        xnT = ctx.sb_at("xnT", [128, NC, T], BF16, 64 * KB)
        qpT = ctx.sb_at("qpT", [128, NC, T], BF16, 96 * KB)
        xsl = [ctx.sb_at(f"xsl{i}", [128, NC, 512], BF16, 96 * KB + 8 * KB * i)
               for i in range(2)]
        sqsl = ctx.sb_at("sqsl", [128, NC, 512], BF16, 112 * KB)
        hT = ctx.sb_at("hT", [128, 4, 512], BF16, 120 * KB)
        dump = ctx.sb_at("dump", [128, T], BF16, 208 * KB)[:, :]
        vtm = ctx.sb_at("vtm", [128, NT, D], BF16, 128 * KB)
        acc = ctx.sb_at("acc", [128, NT, D], BF16, 128 * KB)
        wqk = ctx.sb_at("wqk", [128, NC, D], BF16, 160 * KB)
        w1q = [ctx.sb_at(f"w1q{i}", [128, NC, 512], BF16, 160 * KB + 8 * KB * i)
               for i in range(2)]
        w2q = [ctx.sb_at(f"w2q{i}", [128, 4, D], BF16, 176 * KB + 8 * KB * i)
               for i in range(2)]
        ramps = ctx.sb_at("ramps", [128, 2 * T], F32, 176 * KB)

        ctx.ln_xt = ctx.sb("ln_xt", [128, 2, D], F32)
        ctx.ln_xnb = ctx.sb("ln_xnb", [128, 2, D], BF16)
        ctx.ln_st = ctx.sb("ln_st", [128, 8], F32)
        ctx.ident = ctx.sb("ident", [128, 128], BF16)
        ctx.identf = ctx.sb("identf", [128, 128], F32)
        ctx.onescol = ctx.sb("onescol", [1, 128], BF16)
        ctx.ones_bcol = ctx.sb("ones_bcol", [128, 1], BF16)
        ctx.scr128 = ctx.sb("scr128", [128, 128], F32)
        r1 = ramps[:, 0:T]
        r2 = ramps[:, T:2 * T]
        triu = ctx.sb("triu", [128, 128], F32)[:, :]
        stats = ctx.sb("stats", [128, 80], F32)
        small = ctx.sb("small", [1, 160], F32)
        smallb = ctx.sb("smallb", [1, 64], BF16)
        colbf = ctx.sb("colbf", [128, 1], BF16)
        ett = ctx.sb("ett", [128, 2, 128], BF16)
        sstage = ctx.sb("sstage", [128, 2, D], BF16)
        wgst = ctx.sb("wgst", [128, NC, 17], BF16)
        wgf = ctx.sb("wgf", [128, NC, 8], F32)
        swcol = ctx.sb("swcol", [17, 1], F32)
        swcolb = ctx.sb("swcolb", [17, 1], BF16)
        swb = ctx.sb("swb", [1, 17], BF16)
        stg17 = ctx.sb("stg17", [17, 512], F32)
        sqrow = ctx.sb("sqrow", [1, 512], F32)
        nmub = ctx.sb("nmub", [1, 512], BF16)
        rtt = ctx.sb("rtt", [128, 4, 18], F32)
        rti = ctx.sb("rti", [128, 8, 8], F32)
        rtu = ctx.sb("rtu", [128, 8], U32)
        RI = [ctx.sb(f"RI{k}", [128, 128], F32) for k in range(2)]
        RW = [ctx.sb(f"RW{k}", [128, 128], F32) for k in range(2)]
        RIo = ctx.sb("RIo", [128, 128], F32)
        RWo = [ctx.sb(f"RWo{k}", [128, 128], F32) for k in range(2)]
        rsc = ctx.sb("rsc", [128, 4, 128], F32)
        onesf = ctx.sb("onesf", [128, 128], F32)
        rcols = ctx.sb("rcols", [128, 4], F32)
        rrow = ctx.sb("rrow", [1, 3, 128], F32)
        cmbo = ctx.sb("cmbo", [128, 128], F32)
        cmbs = ctx.sb("cmbs", [16, 128], F32)
        cmbc = ctx.sb("cmbc", [128, 8, 16], F32)
        self_sel = ctx.sb("self_sel", [128, 16], F32)
        b1e = ctx.sb("b1e", [128, 32], F32)
        b2row = ctx.sb_at("b2row", [1, 2, D], BF16, 204 * KB)
        mtmp = ctx.sb_at("mtmp", [128, 2, D], BF16, 200 * KB)
        rtmp = ctx.sb_at("rtmp", [128, 2, D], F32, 192 * KB)
        ctx.actdump = rtmp

        p_a = ctx.ps("p_a", [128, T], F32)
        p_b = ctx.ps("p_b", [128, 1024], F32)
        ctx.p_c = ctx.ps("p_c", [128, 512], F32)
        p_c = ctx.p_c
        ctx.p_tb = ctx.ps("p_tb", [128, 512], BF16)
        p_tb = ctx.p_tb

        emit_const_gen(S, ctx)
        S.op("gpsimd", lambda g: g.iota(
            r1, pattern=[[-128, NT], [1, 128]], base=0, channel_multiplier=-1,
            allow_small_or_imprecise_dtypes=True), inc=pool)
        S.op("gpsimd", lambda g: g.iota(
            r2, pattern=[[-128, NT], [-1, 128]], base=0, channel_multiplier=1,
            allow_small_or_imprecise_dtypes=True), inc=pool)
        g2 = S.op("gpsimd", lambda g: g.iota(
            triu, pattern=[[1, 128]], base=0, channel_multiplier=-1,
            allow_small_or_imprecise_dtypes=True), inc=pool)

        chain(S, "vector", [
            lambda v: v.tensor_scalar(triu, triu, 0.0, None, ALU.is_gt),
            lambda v: v.tensor_scalar(triu, triu, -1e30, None, ALU.mult),
        ], wait=[(pool, g2)])
        S.op("vector", lambda v: v.memset(onesf[:], 1.0), inc=ev)
        S.op("gpsimd", lambda g: g.dma_start(self_sel[:], sel_d.ap()), inc=ld)
        S.barrier()

        emit_ln(S, ctx, xb_d, xnT)
        S.barrier()

        for cc in range(NC):
            S.op("gpsimd", lambda g, cc=cc: g.dma_start(
                wqk[:, cc, :],
                wqk_d.ap().rearrange("(c p) d -> c p d", p=128)[cc]), inc=ld)
        S.op("gpsimd", lambda g: g.dma_start(stats[0:1, 69:70], slope_d.ap()),
             inc=ld)
        S.barrier()
        emit_proj_T(S, ctx, wqk, xnT, qpT, scale=1.0 / 32.0)
        S.barrier()
        for cc in range(NC):
            S.op("gpsimd", lambda g, cc=cc: g.dma_start(
                wqk[:, cc, :],
                wvT_d.ap().rearrange("(c p) d -> c p d", p=128)[cc]), inc=ld)
        S.barrier()
        for tb in range(NT):
            for fb in range(2):
                pairs = [(xnT[:, cc, 128 * tb:128 * (tb + 1)],
                          wqk[:, cc, 512 * fb:512 * (fb + 1)])
                         for cc in range(NC)]
                v = mm_chain(S, ctx, "p_c", p_c[:, 0:512], pairs)
                e = S.op("scalar", lambda sc, tb=tb, fb=fb: sc.activation(
                    vtm[:, tb, 512 * fb:512 * (fb + 1)], p_c[:, 0:512],
                    AF.Identity), wait=[(pe, v)], inc=act)
                ctx.set_guard("p_c", act, e)
        S.barrier()

        emit_bcast_col(S, ctx, stats[:, 68:69], stats[0:1, 69:70], smallb[:, 0:1])

        chain(S, "vector", [
            lambda v: v.tensor_scalar(r1, r1, stats[:, 68:69], None, ALU.mult),
            lambda v: v.tensor_scalar(r2, r2, stats[:, 68:69], None, ALU.mult),
        ], wait=[(ev, ev.n)])
        S.barrier()

        for qb in range(NT):
            kw = 128 * (qb + 1)
            v = None
            for fb in range(cdiv(kw, 512)):
                f0, f1 = 512 * fb, min(512 * (fb + 1), kw)
                pairs = [(qpT[:, cc, 128 * qb:128 * (qb + 1)],
                          xnT[:, cc, f0:f1]) for cc in range(NC)]
                v = mm_chain(S, ctx, "p_a" if fb == 0 else "p_a_x",
                             p_a[:, f0:f1], pairs)

            def alibi_adds(vv, qb=qb):
                for kb in range(qb + 1):
                    vv.tensor_tensor(p_a[:, 128 * kb:128 * (kb + 1)],
                                     p_a[:, 128 * kb:128 * (kb + 1)],
                                     r1[:, 128 * (qb - kb):128 * (qb - kb) + 128],
                                     ALU.add)
                return vv.tensor_tensor(p_a[:, 128 * qb:128 * (qb + 1)],
                                        p_a[:, 128 * qb:128 * (qb + 1)], triu,
                                        ALU.add)

            e = chain(S, "vector", [
                alibi_adds,
                lambda vv, qb=qb, kw=kw: vv.reduce_max(
                    stats[:, qb:qb + 1], p_a[:, 0:kw], axis=AX.X),
                lambda vv, qb=qb: vv.tensor_scalar(
                    stats[:, 32:33], stats[:, qb:qb + 1], -1.0, None,
                    ALU.mult),
            ], wait=[(pe, v)])
            a = S.op("scalar", lambda sc, qb=qb, kw=kw: sc.activation(
                dump[:, 0:kw], p_a[:, 0:kw], AF.Exp, bias=stats[:, 32:33],
                accum_out=stats[:, 16 + qb:17 + qb]), wait=[(ev, e)], inc=act)
            ctx.set_guard("p_a", act, a)
        S.barrier()

        pm = S.op("vector", lambda v: v.reduce_max(stats[:, 35:36],
                                                   stats[:, 0:16], axis=AX.X),
                  inc=ev)
        cv = S.op("vector", lambda v: v.tensor_copy(colbf[:], stats[:, 35:36]),
                  wait=[(ev, pm)], inc=ev)
        pv = S.op("tensor", lambda t: t.transpose(
            p_tb[0:1, 0:128], colbf[:], ctx.ident[:]),
            wait=[(ev, cv)] + ctx.guard("p_tb"), inc=pe)
        c2 = S.op("vector", lambda v: v.tensor_copy(small[:, 0:128],
                                                    p_tb[0:1, 0:128]),
                  wait=[(pe, pv)], inc=ev)
        ctx.set_guard("p_tb", ev, c2)
        rm = S.op("vector", lambda v: v.reduce_max(small[:, 128:129],
                                                   small[:, 0:128], axis=AX.X),
                  wait=[(ev, ev.n)], inc=ev)
        emit_bcast_col(S, ctx, stats[:, 33:34], small[:, 128:129],
                       smallb[:, 1:2], wait=[(ev, rm)])
        mk1 = chain(S, "vector", [
            lambda v: v.tensor_scalar(stats[:, 34:35], stats[:, 33:34], -1.0,
                                      None, ALU.mult),
            lambda v: v.reciprocal(stats[:, 36:52], stats[:, 16:32]),
        ], wait=[(ev, ev.n)])
        fa = S.op("scalar", lambda sc: sc.activation(
            stats[:, 52:68], stats[:, 0:16], AF.Exp, scale=-1.0,
            bias=stats[:, 33:34]), wait=[(ev, mk1)], inc=act)
        S.op("vector", lambda v: v.tensor_tensor(
            stats[:, 52:68], stats[:, 52:68], stats[:, 36:52], ALU.mult),
            wait=[(act, fa)], inc=ev)
        S.barrier()

        if DBG:
            S.op("sync", lambda g: g.dma_start(
                dbg["d_xnT"].ap(), xnT.ap().rearrange("p a b -> p (a b)")),
                inc=st_c)
            S.op("sync", lambda g: g.dma_start(
                dbg["d_qpT"].ap(), qpT.ap().rearrange("p a b -> p (a b)")),
                inc=st_c)
            S.op("sync", lambda g: g.dma_start(
                dbg["d_vtm"].ap(), vtm.ap().rearrange("p a b -> p (a b)")),
                inc=st_c)
            S.op("sync", lambda g: g.dma_start(dbg["d_stats"].ap(),
                                               stats.ap()), inc=st_c)
            S.barrier()

        ett_war = [0, 0]
        sst_war = [0, 0]
        for qb in range(NT):
            pv_last = None
            for kb in range(qb + 1):
                bi = kb % 2
                pairs = [(xnT[:, cc, 128 * kb:128 * (kb + 1)],
                          qpT[:, cc, 128 * qb:128 * (qb + 1)])
                         for cc in range(NC)]
                v = mm_chain(S, ctx, "p_c", p_c[:, 0:128], pairs)
                e = S.op("vector", lambda vv, qb=qb, kb=kb: vv.tensor_tensor(
                    p_c[:, 0:128], p_c[:, 0:128],
                    r2[:, 128 * (qb - kb):128 * (qb - kb) + 128], ALU.add),
                    wait=[(pe, v)], inc=ev)
                a = S.op("scalar", lambda sc, bi=bi: sc.activation(
                    ett[:, bi, :], p_c[:, 0:128], AF.Exp,
                    bias=stats[:, 34:35]),
                    wait=[(ev, e), (pe, ett_war[bi])], inc=act)
                ctx.set_guard("p_c", act, a)
                dep = [(act, a)]
                if kb == qb:
                    g = S.op("gpsimd", lambda gg, bi=bi: gg.affine_select(
                        ett[:, bi, :], ett[:, bi, :], pattern=[[1, 128]],
                        compare_op=ALU.is_ge, fill=0.0, base=0,
                        channel_multiplier=-1), wait=[(act, a)], inc=pool)
                    dep = [(pool, g)]
                gb = ctx.guard("p_b") if kb == 0 else []
                for fb in range(2):
                    pv_last = S.op("tensor", lambda t, bi=bi, kb=kb, fb=fb,
                                   qb=qb:
                                   t.matmul(p_b[:, 512 * fb:512 * (fb + 1)],
                                            ett[:, bi, :],
                                            vtm[:, kb, 512 * fb:512 * (fb + 1)],
                                            start=(kb == 0), stop=(kb == qb)),
                                   wait=(dep + gb if fb == 0 else ()), inc=pe)
                ett_war[bi] = pv_last
            si = qb % 2
            e = S.op("scalar", lambda sc, qb=qb, si=si: sc.activation(
                sstage[:, si, :], p_b[:, 0:1024], AF.Identity,
                scale=stats[:, 52 + qb:53 + qb]),
                wait=[(pe, pv_last), (st_c, sst_war[si])], inc=act)
            ctx.set_guard("p_b", act, e)
            sst_war[si] = S.op("sync", lambda g, qb=qb, si=si: g.dma_start(
                s_scr.ap().rearrange("(n p) d -> n p d", p=128)[qb],
                sstage[:, si, :]), wait=[(act, e)], inc=st_c)
        S.barrier()

        lw = S.op("gpsimd", lambda g: g.dma_start(
            wgf.ap().rearrange("p c e -> p (c e)"), wg_d.ap()), inc=ld)

        wv_ = chain(S, "vector", [
            lambda v: v.tensor_copy(wgst[:, :, 1:9], wgf[:]),
            lambda v: v.tensor_tensor(wgst[:, :, 9:17], wgf[:],
                                      wgst[:, :, 1:9], ALU.subtract),
            lambda v: v.memset(wgst[:, :, 0:1], 1.0),
        ], wait=[(ld, lw)])
        pairs = [(wgst[:, cc, :], ctx.ones_bcol[:]) for cc in range(NC)]
        v = mm_chain(S, ctx, "p_c", p_c[0:17, 0:1], pairs, wait=[(ev, wv_)])
        e = S.op("vector", lambda vv: vv.tensor_copy(swcol[:], p_c[0:17, 0:1]),
                 wait=[(pe, v)], inc=ev)
        ctx.set_guard("p_c", ev, e)
        e2 = S.op("vector", lambda vv: vv.tensor_copy(swcolb[:], swcol[:]),
                  wait=[(ev, e)], inc=ev)
        pv = S.op("tensor", lambda t: t.transpose(
            p_tb[0:1, 0:17], swcolb[:], ctx.ident[0:17, 0:17]),
            wait=[(ev, e2)] + ctx.guard("p_tb"), inc=pe)

        e3 = chain(S, "vector", [
            lambda vv: vv.tensor_copy(swb[:], p_tb[0:1, 0:17]),
            lambda vv: vv.memset(swb[:, 0:1], 0.0),
        ], wait=[(pe, pv)])
        ctx.set_guard("p_tb", ev, e3)

        xTr = xTf_d.ap().rearrange("(c p) t -> p c t", p=128)
        xsl_war = [[], []]
        tile_j = 0
        for fb in range(TF // 512):
            bi = fb % 2
            for cc in range(NC):
                lv = S.op("gpsimd", lambda g, fb=fb, bi=bi, cc=cc: g.dma_start(
                    xsl[bi][:, cc, :], xTr[:, cc, 512 * fb:512 * (fb + 1)]),
                    wait=(xsl_war[bi] if cc == 0 else ()), inc=ld)
            sq = S.op("vector", lambda v, bi=bi: v.tensor_mul(
                sqsl.ap(), xsl[bi].ap(), xsl[bi].ap()), wait=[(ld, lv)], inc=ev)
            pairs = [(wgst[:, cc, :], xsl[bi][:, cc, :]) for cc in range(NC)]
            v1 = mm_chain(S, ctx, "p_a", p_a[0:17, 0:512], pairs,
                          wait=[(ld, lv)], keep_open=True)
            pairs = [(ctx.ones_bcol[:], sqsl[:, cc, :]) for cc in range(NC)]
            v2 = mm_chain(S, ctx, "p_a2", p_a[0:1, 512:1024], pairs,
                          wait=[(ev, sq)])
            xsl_war[bi] = [(pe, v2), (ev, sq)]
            nv = S.op("vector", lambda v: v.tensor_scalar(
                nmub[:], p_a[0:1, 0:512], -1.0 / D, None, ALU.mult),
                wait=[(pe, v1)], inc=ev)
            v3 = S.op("tensor", lambda t: t.matmul(
                p_a[0:17, 0:512], swb[:], nmub[:], start=False, stop=True),
                wait=[(ev, nv)], inc=pe)
            e1_ = S.op("vector", lambda v: v.tensor_copy(stg17[:],
                                                         p_a[0:17, 0:512]),
                       wait=[(pe, v3)], inc=ev)
            e2_ = S.op("vector", lambda v: v.tensor_copy(sqrow[:],
                                                         p_a[0:1, 512:1024]),
                       wait=[(pe, v2)], inc=ev)
            ctx.set_guard("p_a", ev, e1_)
            ctx.set_guard("p_a2", ev, e2_)
            for j4 in range(4):
                r = tile_j % 4
                t1 = S.op("tensor", lambda t, j4=j4: t.transpose(
                    p_c[:, 0:17], stg17[:, 128 * j4:128 * (j4 + 1)],
                    ctx.identf[0:17, 0:17]),
                    wait=[(ev, e1_)] + ctx.guard("p_c"),
                    inc=pe)
                c1 = S.op("vector", lambda v, r=r: v.tensor_copy(
                    rtt[:, r, 0:17], p_c[:, 0:17]), wait=[(pe, t1)], inc=ev)
                ctx.set_guard("p_c", ev, c1)
                t2 = S.op("tensor", lambda t, j4=j4: t.transpose(
                    p_c[:, 0:1], sqrow[:, 128 * j4:128 * (j4 + 1)],
                    ctx.identf[0:1, 0:1]),
                    wait=[(ev, e2_)] + ctx.guard("p_c"),
                    inc=pe)
                c2_ = S.op("vector", lambda v, r=r: v.tensor_copy(
                    rtt[:, r, 17:18], p_c[:, 0:1]), wait=[(pe, t2)], inc=ev)
                ctx.set_guard("p_c", ev, c2_)

                m1 = chain(S, "vector", [
                    lambda v, r=r: v.tensor_tensor(
                        rti[:, 0, 0:8], rtt[:, r, 1:9], rtt[:, r, 9:17],
                        ALU.add),
                    lambda v, r=r: v.tensor_scalar(
                        rti[:, 1, 0:1], rtt[:, r, 0:1], 1.0 / D, None,
                        ALU.mult),
                    lambda v, r=r: v.tensor_scalar(
                        rti[:, 1, 1:2], rtt[:, r, 17:18], 1.0 / D, None,
                        ALU.mult),
                    lambda v: v.tensor_tensor(
                        rti[:, 1, 2:3], rti[:, 1, 0:1], rti[:, 1, 0:1],
                        ALU.mult),
                    lambda v: v.tensor_tensor(
                        rti[:, 1, 3:4], rti[:, 1, 1:2], rti[:, 1, 2:3],
                        ALU.subtract),
                    lambda v: v.tensor_scalar(
                        rti[:, 1, 3:4], rti[:, 1, 3:4], LN_EPS, None,
                        ALU.add),
                ], wait=[(ev, c1), (ev, c2_)])
                a1 = S.op("scalar", lambda sc: sc.sqrt(rti[:, 1, 4:5],
                                                       rti[:, 1, 3:4]),
                          wait=[(ev, m1)], inc=act)

                m2 = chain(S, "vector", [
                    lambda v: v.reciprocal(rti[:, 1, 5:6], rti[:, 1, 4:5]),
                    lambda v: v.tensor_scalar(rti[:, 2, 0:8], rti[:, 0, 0:8],
                                              rti[:, 1, 5:6], None, ALU.mult),
                    lambda v: v.reduce_max(rti[:, 1, 6:7], rti[:, 2, 0:8],
                                           axis=AX.X),
                    lambda v: v.tensor_scalar(rti[:, 1, 7:8], rti[:, 1, 6:7],
                                              -1.0, None, ALU.mult),
                ], wait=[(act, a1)])
                a2 = S.op("scalar", lambda sc: sc.activation(
                    rti[:, 3, 0:8], rti[:, 2, 0:8], AF.Exp,
                    bias=rti[:, 1, 7:8], accum_out=rti[:, 4, 0:1]),
                    wait=[(ev, m2)], inc=act)

                m3 = chain(S, "vector", [
                    lambda v: v.max(rti[:, 5, 0:8], rti[:, 2, 0:8]),
                    lambda v: v.max_index(rtu[:, 0:8], rti[:, 5, 0:8],
                                          rti[:, 2, 0:8]),
                ], wait=[(act, a2)])
                a3 = S.op("scalar", lambda sc: sc.activation(
                    rti[:, 6, 0:2], rti[:, 5, 0:2], AF.Exp,
                    bias=rti[:, 1, 7:8]), wait=[(ev, m3)], inc=act)

                def t4_copies(v, j=tile_j):
                    v.tensor_copy(RI[0][:, j:j + 1], rtu[:, 0:1])
                    v.tensor_copy(RI[1][:, j:j + 1], rtu[:, 1:2])
                    v.tensor_copy(RW[0][:, j:j + 1], rti[:, 7, 0:1])
                    return v.tensor_copy(RW[1][:, j:j + 1], rti[:, 7, 1:2])

                chain(S, "vector", [
                    lambda v: v.tensor_tensor(rti[:, 4, 1:2], rti[:, 6, 0:1],
                                              rti[:, 6, 1:2], ALU.add),
                    lambda v: v.tensor_scalar(rti[:, 4, 2:3], rti[:, 4, 0:1],
                                              1e-9, rti[:, 4, 1:2], ALU.mult,
                                              ALU.add),
                    lambda v: v.reciprocal(rti[:, 4, 3:4], rti[:, 4, 2:3]),
                    lambda v: v.tensor_scalar(rti[:, 7, 0:2], rti[:, 6, 0:2],
                                              rti[:, 4, 3:4], None, ALU.mult),
                    t4_copies,
                ], wait=[(act, a3)])
                tile_j += 1
        S.barrier()

        if DBG:
            S.op("sync", lambda g: g.dma_start(dbg["d_s"].ap(), s_scr.ap()),
                 inc=st_c)
            S.op("sync", lambda g: g.dma_start(dbg["d_ri"][:, 0:128],
                                               RI[0][:]), inc=st_c)
            S.op("sync", lambda g: g.dma_start(dbg["d_ri"][:, 128:256],
                                               RI[1][:]), inc=st_c)
            S.op("sync", lambda g: g.dma_start(dbg["d_rw"][:, 0:128],
                                               RW[0][:]), inc=st_c)
            S.op("sync", lambda g: g.dma_start(dbg["d_rw"][:, 128:256],
                                               RW[1][:]), inc=st_c)
            S.barrier()

        for k in range(2):
            emit_tr_f32(S, ctx, RWo[k][:], RW[k][:], 128)
        for e_ in range(8):
            for k in range(2):
                tv = emit_tr_f32(S, ctx, RIo[:], RI[k][:], 128)
                mv = S.op("vector", lambda v, e_=e_: v.tensor_scalar(
                    rsc[:, 0, :], RIo[:], float(e_), None, ALU.is_equal),
                    wait=[(ev, tv)], inc=ev)
                sv = S.op("vector", lambda v: v.tensor_tensor_scan(
                    rsc[:, 1, :], onesf[:], rsc[:, 0, :], 0.0,
                    ALU.mult, ALU.add), wait=[(ev, mv)], inc=ev)
                cv1 = S.op("vector", lambda v: v.tensor_copy(
                    colbf[:], rsc[:, 1, 127:128]), wait=[(ev, sv)], inc=ev)
                tv1 = S.op("tensor", lambda t: t.transpose(
                    p_tb[0:1, 0:128], colbf[:], ctx.ident[:]),
                    wait=[(ev, cv1)] + ctx.guard("p_tb"), inc=pe)
                cv2 = S.op("vector", lambda v: v.tensor_copy(
                    rrow[:, 0, :], p_tb[0:1, 0:128]), wait=[(pe, tv1)], inc=ev)
                ctx.set_guard("p_tb", ev, cv2)

                rv = chain(S, "vector", [
                    lambda v: v.tensor_tensor_scan(
                        rrow[:, 1, :], onesf[0:1, :], rrow[:, 0, :], 0.0,
                        ALU.mult, ALU.add),
                    lambda v: v.tensor_tensor(rrow[:, 2, :], rrow[:, 1, :],
                                              rrow[:, 0, :], ALU.subtract),
                ], wait=[(ev, cv2)])
                if k == 0:
                    S.op("vector", lambda v, e_=e_: v.tensor_copy(
                        small[:, 140 + e_:141 + e_], rrow[:, 1, 127:128]),
                        wait=[(ev, rv)], inc=ev)
                    thv = S.op("vector", lambda v: v.tensor_scalar(
                        rrow[:, 0, :], onesf[0:1, :], 2049.0, None, ALU.mult),
                        wait=[(ev, rv)], inc=ev)
                else:
                    thv = chain(S, "vector", [
                        lambda v, e_=e_: v.tensor_scalar(
                            small[:, 156:157], small[:, 140 + e_:141 + e_],
                            2048.0, None, ALU.min),
                        lambda v: v.tensor_scalar(
                            small[:, 156:157], small[:, 156:157], -1.0,
                            2049.0, ALU.mult, ALU.add),
                        lambda v: v.tensor_scalar(
                            rrow[:, 0, :], onesf[0:1, :], small[:, 156:157],
                            None, ALU.mult),
                    ], wait=[(ev, rv)])
                t1 = S.op("tensor", lambda t: t.transpose(
                    p_c[:, 0:1], rrow[0:1, 2, 0:128], ctx.identf[0:1, 0:1]),
                    wait=[(ev, rv)] + ctx.guard("p_c"), inc=pe)
                c4 = S.op("vector", lambda v: v.tensor_copy(
                    rcols[:, 0:1], p_c[:, 0:1]), wait=[(pe, t1)], inc=ev)
                ctx.set_guard("p_c", ev, c4)
                t2 = S.op("tensor", lambda t: t.transpose(
                    p_c[:, 0:1], rrow[0:1, 0, 0:128], ctx.identf[0:1, 0:1]),
                    wait=[(ev, thv)] + ctx.guard("p_c"), inc=pe)
                c5 = S.op("vector", lambda v: v.tensor_copy(
                    rcols[:, 1:2], p_c[:, 0:1]), wait=[(pe, t2)], inc=ev)
                ctx.set_guard("p_c", ev, c5)

                steps = [
                    lambda v: v.tensor_scalar(rsc[:, 1, :], rsc[:, 1, :],
                                              rcols[:, 0:1], None, ALU.add),
                    lambda v: v.tensor_scalar(rsc[:, 1, :], rsc[:, 1, :],
                                              rcols[:, 1:2], None, ALU.is_lt),
                    lambda v: v.tensor_tensor(rsc[:, 1, :], rsc[:, 1, :],
                                              rsc[:, 0, :], ALU.mult),
                    lambda v, k=k: v.tensor_tensor(rsc[:, 1, :], rsc[:, 1, :],
                                                   RWo[k][:], ALU.mult),
                ]
                if k == 0:
                    steps.append(lambda v: v.tensor_copy(rsc[:, 3, :],
                                                         rsc[:, 1, :]))
                else:
                    steps.append(lambda v: v.tensor_tensor(
                        cmbo[:], rsc[:, 1, :], rsc[:, 3, :], ALU.add))
                cm = chain(S, "vector", steps, wait=[(ev, c4), (ev, c5)])
            pv1 = S.op("tensor", lambda t: t.matmul(
                p_c[0:16, 0:128], self_sel[:], cmbo[:], start=True, stop=True),
                wait=[(ev, cm)] + ctx.guard("p_c"), inc=pe)
            ce = S.op("vector", lambda v: v.tensor_copy(cmbs[:],
                                                        p_c[0:16, 0:128]),
                      wait=[(pe, pv1)], inc=ev)
            ctx.set_guard("p_c", ev, ce)
            t3 = S.op("tensor", lambda t: t.transpose(
                p_c[:, 0:16], cmbs[:], ctx.identf[0:16, 0:16]),
                wait=[(ev, ce)] + ctx.guard("p_c"), inc=pe)
            ce2 = S.op("vector", lambda v, e_=e_: v.tensor_copy(
                cmbc[:, e_, :], p_c[:, 0:16]), wait=[(pe, t3)], inc=ev)
            ctx.set_guard("p_c", ev, ce2)
        S.barrier()

        if DBG:
            S.op("sync", lambda g: g.dma_start(
                dbg["d_cmbc"].ap(), cmbc.ap().rearrange("p a b -> p (a b)")),
                inc=st_c)
            S.barrier()

        S.op("vector", lambda v: v.memset(acc.ap().rearrange("p a b -> p (a b)"),
                                          0.0), inc=ev)
        S.barrier()

        wload = [0, 0]
        wfree = [0, 0]
        b2war = [0, 0]

        def load_wq(e_, q8):
            par = (e_ * 8 + q8) % 2
            off1 = (e_ * 1024) * 4096 + 512 * q8
            for cc in range(NC):
                S.op("gpsimd", lambda g, off1=off1, par=par, cc=cc:
                     g.dma_start(w1q[par][:, cc, :],
                                 AP(w1_d, off1 + cc * 128 * 4096,
                                    [[4096, 128], [1, 512]])),
                     wait=([(pe, wfree[par])] if cc == 0 else ()), inc=ld)
            off2 = (e_ * 4096 + 512 * q8) * 1024
            lv = None
            for cc in range(4):
                lv = S.op("gpsimd", lambda g, off2=off2, par=par, cc=cc:
                          g.dma_start(w2q[par][:, cc, :],
                                      AP(w2_d, off2 + cc * 128 * 1024,
                                         [[1024, 128], [1, 1024]])), inc=ld)
            wload[par] = lv

        load_wq(0, 0)
        mt_war = [0, 0]
        hT_war = 0
        b1war = 0
        for e_ in range(8):
            blv = S.op("gpsimd", lambda g, e_=e_: g.dma_start(
                b1e.ap(), AP(b1_d, e_ * 4096, [[32, 128], [1, 32]])),
                wait=[(act, b1war)], inc=ld)
            b2lv = S.op("gpsimd", lambda g, e_=e_: g.dma_start(
                b2row[:, e_ % 2, :], AP(b2_d, e_ * D, [[D, 1], [1, D]])),
                wait=[(pe, b2war[e_ % 2])], inc=ld)
            for q8 in range(8):
                par = (e_ * 8 + q8) % 2
                if e_ * 8 + q8 + 1 < 64:
                    load_wq((e_ * 8 + q8 + 1) // 8, (e_ * 8 + q8 + 1) % 8)
                for TB in range(4):
                    t0 = 512 * TB
                    gval = None
                    for hc in range(4):
                        pairs = [(w1q[par][:, cc, 128 * hc:128 * (hc + 1)],
                                  xnT[:, cc, t0:t0 + 512])
                                 for cc in range(NC)]
                        v = mm_chain(S, ctx, "p_c", p_c[:, 0:512], pairs,
                                     wait=[(ld, wload[par])])
                        gval = S.op("scalar", lambda sc, hc=hc, q8=q8, par=par:
                                    sc.activation(
                                        hT[:, hc, :], p_c[:, 0:512],
                                        AF.Gelu_apprx_tanh,
                                        bias=b1e[:, 4 * q8 + hc:4 * q8 + hc + 1]),
                                    wait=[(pe, v), (pe, hT_war),
                                          (ld, blv)], inc=act)
                        ctx.set_guard("p_c", act, gval)
                    for tj in range(4):
                        tb = 4 * TB + tj
                        gb = ctx.guard("p_b")
                        vv = None
                        for hc in range(4):
                            for fb in range(2):
                                isf = (hc == 0)
                                isl = (hc == 3 and q8 != 0)
                                w = ([(act, gval), (ld, b2lv)] + gb
                                     if hc == 0 and fb == 0 else ())
                                vv = S.op("tensor",
                                          lambda t, tj=tj, hc=hc, fb=fb,
                                          par=par, isf=isf, isl=isl:
                                          t.matmul(
                                              p_b[:, 512 * fb:512 * (fb + 1)],
                                              hT[:, hc, 128 * tj:128 * (tj + 1)],
                                              w2q[par][:, hc,
                                                       512 * fb:512 * (fb + 1)],
                                              start=isf, stop=isl),
                                          wait=w, inc=pe)
                        if q8 == 0:
                            for fb in range(2):
                                vv = S.op("tensor", lambda t, fb=fb, e_=e_:
                                          t.matmul(
                                              p_b[:, 512 * fb:512 * (fb + 1)],
                                              ctx.onescol[:],
                                              b2row[:, e_ % 2,
                                                    512 * fb:512 * (fb + 1)],
                                              start=False, stop=True),
                                          inc=pe)
                        mi = tb % 2
                        eva = S.op("scalar", lambda sc, mi=mi, e_=e_, tb=tb:
                                   sc.activation(
                                       mtmp[:, mi, :], p_b[:, 0:1024],
                                       AF.Identity,
                                       scale=cmbc[:, e_, tb:tb + 1]),
                                   wait=[(pe, vv), (pool, mt_war[mi])],
                                   inc=act)
                        ctx.set_guard("p_b", act, eva)
                        mt_war[mi] = S.op("gpsimd", lambda g, mi=mi, tb=tb:
                                          g.tensor_add(acc[:, tb, :],
                                                       acc[:, tb, :],
                                                       mtmp[:, mi, :]),
                                          wait=[(act, eva)], inc=pool)
                    hT_war = S.C["pe"].n
                wfree[par] = S.C["pe"].n
            b2war[e_ % 2] = S.C["pe"].n
            b1war = S.C["act"].n
        S.barrier()

        if DBG:
            S.op("sync", lambda g: g.dma_start(
                dbg["d_acc"].ap(), acc.ap().rearrange("p a b -> p (a b)")),
                inc=st_c)
            S.barrier()

        xr = xb_d.ap().rearrange("(n p) d -> n p d", p=128)
        sr = s_scr.ap().rearrange("(n p) d -> n p d", p=128)
        x1r = x1_d.ap().rearrange("(n p) d -> n p d", p=128)
        res_war = [0, 0]
        for i in range(NT):
            bi = i % 2
            lvx = S.op("gpsimd", lambda g, i=i, bi=bi: g.dma_start(
                ctx.ln_xt[:, bi, :], xr[i]), wait=[(st_c, res_war[bi])],
                inc=ld)
            lvs = S.op("gpsimd", lambda g, i=i, bi=bi: g.dma_start(
                ctx.ln_xnb[:, bi, :], sr[i]), inc=ld)
            a = S.op("scalar", lambda sc, i=i, bi=bi: sc.activation(
                rtmp[:, bi, :], acc[:, i, :], AF.Silu),
                wait=[(ld, lvs)], inc=act)

            e = chain(S, "vector", [
                lambda v, bi=bi: v.tensor_tensor(rtmp[:, bi, :],
                                                 rtmp[:, bi, :],
                                                 ctx.ln_xnb[:, bi, :],
                                                 ALU.mult),
                lambda v, bi=bi: v.tensor_tensor(rtmp[:, bi, :],
                                                 rtmp[:, bi, :],
                                                 ctx.ln_xt[:, bi, :],
                                                 ALU.add),
            ], wait=[(act, a), (ld, lvx)])
            res_war[bi] = S.op("sync", lambda g, i=i, bi=bi: g.dma_start(
                x1r[i], rtmp[:, bi, :]), wait=[(ev, e)], inc=st_c)
        S.barrier()

        block = es.enter_context(nc.Block())
        S.emit(block)
    return nc




QTHR = [0.25, 0.75, 1.25, 1.75, 2.5, 2.5, 3.5, 3.5, 5.0, 5.0, 5.0, 5.0]
I3 = 2816
IH = I3 // 2


def emit_quant(S, ctx, out_ap, in_ap, Wd, qt, qacc, qsgn, qsc, wait=()):
    nb = Wd // 16
    iv = in_ap.rearrange("p (a b) -> p a b", b=16)
    qtv = qt.rearrange("p (a b) -> p a b", b=16)
    outv = out_ap.rearrange("p (a b) -> p a b", b=16)
    ev, act = S.C["ev"], S.C["act"]
    r = S.op("vector", lambda v: v.reduce_max(
        qsc[:, 0:nb, 0:1], iv, axis=AX.X, apply_absolute_value=True),
        wait=wait, inc=ev)
    r = chain(S, "vector", [
        lambda v: v.tensor_scalar(qsc[:, 0:nb, 0:1], qsc[:, 0:nb, 0:1],
                                  1.0 / 6.0, 1e-12, ALU.mult, ALU.max),
        lambda v: v.reciprocal(qsc[:, 0:nb, 1:2], qsc[:, 0:nb, 0:1]),
    ], wait=[(ev, r)])
    bv = None
    for b in range(16):
        bv = S.op("vector", lambda v, b=b: v.tensor_tensor(
            qtv[:, 0:nb, b:b + 1], iv[:, 0:nb, b:b + 1], qsc[:, 0:nb, 1:2],
            ALU.mult), wait=([(ev, r)] if b == 0 else ()), inc=ev)
    av = S.op("scalar", lambda sc: sc.activation(qacc[:, 0:Wd], qt[:, 0:Wd],
                                                 AF.Abs),
              wait=[(ev, bv)], inc=act)
    gv = S.op("scalar", lambda sc: sc.activation(qsgn[:, 0:Wd], qt[:, 0:Wd],
                                                 AF.Sign), inc=act)
    steps = [lambda v: v.tensor_scalar(qt[:, 0:Wd], qacc[:, 0:Wd], QTHR[0],
                                       None, ALU.is_gt)]
    for th in QTHR[1:]:
        steps.append(lambda v, th=th: v.scalar_tensor_tensor(
            qt[:, 0:Wd], qacc[:, 0:Wd], th, qt[:, 0:Wd], ALU.is_gt, ALU.add))
    steps.append(lambda v: v.tensor_tensor(qt[:, 0:Wd], qt[:, 0:Wd],
                                           qsgn[:, 0:Wd], ALU.mult))
    steps.append(lambda v: v.tensor_scalar(qsc[:, 0:nb, 1:2],
                                           qsc[:, 0:nb, 0:1], 0.5, None,
                                           ALU.mult))
    sv = chain(S, "vector", steps, wait=[(act, gv)])
    bv = None
    for b in range(16):
        bv = S.op("vector", lambda v, b=b: v.tensor_tensor(
            outv[:, 0:nb, b:b + 1], qtv[:, 0:nb, b:b + 1],
            qsc[:, 0:nb, 1:2], ALU.mult),
            wait=([(ev, sv)] if b == 0 else ()), inc=ev)
    return bv


def build_L1():
    nc = bass.Bass("TRN2", target_bir_lowering=False)

    def din(name, shape, dt=F32):
        return nc.dram_tensor(name, shape, dt, kind="ExternalInput")

    xb_d = din("xb", [T, D])
    kpad_d = din("kpad", [128, 1024])
    wg_d = din("wgate", [I3, D])
    wu_d = din("wup", [I3, D])
    wd_d = din("wdown", [D, I3])
    x2_d = nc.dram_tensor("x2", [T, D], F32, kind="ExternalOutput")
    s_scr = nc.dram_tensor("s_scr", [T, D], BF16, kind="Internal")
    wqg_scr = nc.dram_tensor("wqg_scr", [D, I3], BF16, kind="Internal")
    wqu_scr = nc.dram_tensor("wqu_scr", [D, I3], BF16, kind="Internal")
    wqd_scr = nc.dram_tensor("wqd_scr", [I3, D], BF16, kind="Internal")
    import os as _os
    DBG = _os.environ.get("KDBG", "") == "1"
    if DBG:
        dbg = {
            "d_s1": nc.dram_tensor("d_s1", [T, D], BF16, kind="ExternalOutput"),
            "d_xq": nc.dram_tensor("d_xq", [128, NT * D], BF16,
                                   kind="ExternalOutput"),
            "d_wqg": nc.dram_tensor("d_wqg", [D, I3], BF16,
                                    kind="ExternalOutput"),
            "d_wqd": nc.dram_tensor("d_wqd", [I3, D], BF16,
                                    kind="ExternalOutput"),
            "d_acc2": nc.dram_tensor("d_acc2", [128, NT * D], BF16,
                                     kind="ExternalOutput"),
            "d_qwbf": nc.dram_tensor("d_qwbf", [128, I3], BF16,
                                     kind="ExternalOutput"),
            "d_qwin": nc.dram_tensor("d_qwin", [128, I3], F32,
                                     kind="ExternalOutput"),
        }

    es = contextlib.ExitStack()
    with es:
        ctx = Ctx(nc, es)
        S = Sched(nc, es)
        S.C = dict(ld=S.ctr("ld", 16), st=S.ctr("st", 16), pe=S.ctr("pe", 1),
                   ev=S.ctr("ev", 1), act=S.ctr("act", 1), pool=S.ctr("pool", 1))
        C = S.C
        ld, st_c, pe, ev, act, pool = (C["ld"], C["st"], C["pe"], C["ev"],
                                       C["act"], C["pool"])

        qt = ctx.sb_at("qt", [128, IH], F32, 52 * KB)[:, :]
        qacc = ctx.sb_at("qacc", [128, IH], F32, 58 * KB)[:, :]
        qsgn = ctx.sb_at("qsgn", [128, IH], BF16, 64 * KB)[:, :]
        gate_sb = ctx.sb_at("gate_sb", [128, IH], BF16, 67 * KB)
        h_sb = ctx.sb_at("h_sb", [128, IH], BF16, 70 * KB)
        hqT = ctx.sb_at("hqT", [128, 11, 128], BF16, 73 * KB)
        xn = ctx.sb_at("xn", [128, NT, D], BF16, 76 * KB)
        acc2 = ctx.sb_at("acc2", [128, NT, D], BF16, 76 * KB)
        xrev = ctx.sb_at("xrev", [128, NT, D], BF16, 108 * KB)
        xqT = ctx.sb_at("xqT", [128, NC, T], BF16, 108 * KB)
        xq_tmp = ctx.sb_at("xq_tmp", [128, NT, D], BF16, 140 * KB)
        wgh = ctx.sb_at("wgh", [128, NC, IH], BF16, 140 * KB)
        wuh = ctx.sb_at("wuh", [128, NC, IH], BF16, 162 * KB)
        wdh = ctx.sb_at("wdh", [128, 11, D], BF16, 184 * KB)
        s1 = ctx.sb_at("s1", [128, NT, D], BF16, 172 * KB)
        qwtile = ctx.sb_at("qwtile", [128, I3], F32, 206 * KB)
        qwbf = ctx.sb_at("qwbf", [128, I3], BF16, 217 * KB)

        ctx.ln_xt = ctx.sb("ln_xt", [128, 2, D], F32)
        ctx.ln_xnb = ctx.sb("ln_xnb", [128, 2, D], BF16)
        ctx.ln_st = ctx.sb("ln_st", [128, 8], F32)
        ctx.ident = ctx.sb("ident", [128, 128], BF16)
        ctx.identf = ctx.sb("identf", [128, 128], F32)
        ctx.onescol = ctx.sb("onescol", [1, 128], BF16)
        ctx.ones_bcol = ctx.sb("ones_bcol", [128, 1], BF16)
        ctx.scr128 = ctx.sb("scr128", [128, 128], F32)
        antij = ctx.sb("antij", [128, 128], BF16)
        At = [ctx.sb(f"At{i}", [128, 904], BF16) for i in range(2)]
        rtmp = ctx.sb("rtmp", [128, 2, D], F32)
        ctx.actdump = rtmp
        lnsk = ctx.sb("lnsk", [128, NT, 2], F32)
        stile = ctx.sb("stile", [128, 2, D], BF16)
        ttile = ctx.sb("ttile", [128, 2, 128], BF16)
        qsc = ctx.sb("qsc", [128, IH // 16, 2], F32)

        p_g = ctx.ps("p_g", [128, 1536], F32)
        p_b = ctx.ps("p_b", [128, 1024], F32)
        ctx.p_c = ctx.ps("p_c", [128, 512], F32)
        p_c = ctx.p_c
        ctx.p_tb = ctx.ps("p_tb", [128, 512], BF16)

        emit_const_gen(S, ctx)
        gj = S.op("gpsimd", lambda g: g.iota(
            ctx.scr128[:], pattern=[[1, 128]], base=-127, channel_multiplier=1,
            allow_small_or_imprecise_dtypes=True),
            wait=[(ev, ev.n)], inc=pool)
        chain(S, "vector", [
            lambda v: v.tensor_scalar(ctx.scr128[:], ctx.scr128[:], 0.0, None,
                                      ALU.is_equal),
            lambda v: v.tensor_copy(antij[:], ctx.scr128[:]),
        ], wait=[(pool, gj)])
        S.barrier()

        emit_ln(S, ctx, xb_d, None, xn_keep=xn, stats_keep=lnsk)
        S.barrier()

        for i in range(NT):
            for fb in range(2):
                v = mm_chain(S, ctx, "p_c", p_c[:, 0:512],
                             [(antij[:], xn[:, i, 512 * fb:512 * (fb + 1)])])
                e = S.op("scalar", lambda sc, i=i, fb=fb: sc.activation(
                    xrev[:, i, 512 * fb:512 * (fb + 1)], p_c[:, 0:512],
                    AF.Identity), wait=[(pe, v)], inc=act)
                ctx.set_guard("p_c", act, e)
        S.barrier()

        at_war = [[], []]
        for r in range(128):
            bi = r % 2
            lv = S.op("gpsimd", lambda g, r=r, bi=bi: g.dma_start(
                At[bi][:, 0:897], AP(kpad_d, r * 1024, [[1, 128], [1, 897]])),
                wait=at_war[bi], inc=ld)
            is_diff = (r % 8) < 4
            mcnt = 2 if is_diff else 7
            v = None
            gw = ctx.guard("p_conv")
            for m in range(mcnt):
                nf = (NT - m) * 8
                v = S.op("tensor", lambda t, r=r, m=m, bi=bi, nf=nf:
                         t.matmul(p_b[:, 8 * m:8 * m + nf],
                                  At[bi][:, 128 * m:128 * m + 128],
                                  xrev.ap()[:, 0:NT - m, 8 * r:8 * r + 8],
                                  start=(m == 0), stop=(m == mcnt - 1)),
                         wait=([(ld, lv)] + gw if m == 0 else ()), inc=pe)
            at_war[bi] = [(pe, v)]
            e = S.op("vector", lambda vv, r=r: vv.tensor_copy(
                s1.ap()[:, :, 8 * r:8 * r + 8],
                p_b[:, 0:128].rearrange("p (i h) -> p i h", h=8)),
                wait=[(pe, v)], inc=ev)
            ctx.set_guard("p_conv", ev, e)
        S.barrier()
        for i in range(NT):
            S.op("sync", lambda g, i=i: g.dma_start(
                s_scr.ap().rearrange("(n p) d -> n p d", p=128)[i],
                s1[:, i, :]), inc=st_c)

        qx_war = []
        for i in range(NT):
            bi = i % 2
            lvq = S.op("gpsimd", lambda g, i=i, bi=bi: g.dma_start(
                ctx.ln_xt[:, bi, :],
                xb_d.ap().rearrange("(n p) d -> n p d", p=128)[i]),
                wait=qx_war, inc=ld)
            aq = S.op("scalar", lambda sc, i=i, bi=bi: sc.activation(
                rtmp[:, bi, :], ctx.ln_xt[:, bi, :], AF.Identity,
                bias=lnsk[:, i, 0:1], scale=lnsk[:, i, 1:2]),
                wait=[(ld, lvq)], inc=act)
            qv2 = None
            for hf in range(2):
                qv2 = emit_quant(S, ctx, xq_tmp[:, i, 512 * hf:512 * (hf + 1)],
                                 rtmp[:, bi, 512 * hf:512 * (hf + 1)], 512,
                                 qt, qacc, qsgn, qsc,
                                 wait=[(act, aq)] if hf == 0 else ())
            if i >= 1:
                qx_war = [(ev, qv2)]
        S.barrier()
        for i in range(NT):
            for c in range(NC):
                pv = S.op("tensor", lambda t, i=i, c=c: t.transpose(
                    ctx.p_tb[:, 0:128], xq_tmp[:, i, 128 * c:128 * (c + 1)],
                    ctx.ident[:]), wait=ctx.guard("p_tb"), inc=pe)
                tv = S.op("vector", lambda v, i=i, c=c: v.tensor_copy(
                    xqT[:, c, 128 * i:128 * (i + 1)], ctx.p_tb[:, 0:128]),
                    wait=[(pe, pv)], inc=ev)
                ctx.set_guard("p_tb", ev, tv)
        S.barrier()

        if DBG:
            S.op("sync", lambda g: g.dma_start(dbg["d_xq"].ap(),
                 xq_tmp.ap().rearrange("p a b -> p (a b)")), inc=st_c)
            S.op("sync", lambda g: g.dma_start(dbg["d_s1"].ap(), s_scr.ap()),
                 inc=st_c)
            S.barrier()

        qw_state = {"tw": [[], []], "war": []}

        def quant_w(w_d, scr, ntiles, Wd):
            tw = qw_state["tw"]
            qw_war = qw_state["war"]
            for nt in range(ntiles):
                lv = S.op("gpsimd", lambda g, nt=nt, Wd=Wd, w_d=w_d:
                          g.dma_start(qwtile[:, 0:Wd],
                                      AP(w_d, nt * 128 * Wd,
                                         [[Wd, 128], [1, Wd]])),
                          wait=qw_war, inc=ld)
                qv = None
                nhf = cdiv(Wd, 512)
                for hf in range(nhf):
                    f0, f1 = 512 * hf, min(512 * (hf + 1), Wd)
                    qv = emit_quant(S, ctx, qwbf[:, f0:f1], qwtile[:, f0:f1],
                                    f1 - f0, qt, qacc, qsgn, qsc,
                                    wait=[(ld, lv)] if hf == 0 else ())
                qw_war = [(ev, qv)]
                if DBG and nt == 0 and scr is wqg_scr:
                    S.op("sync", lambda g, Wd=Wd: g.dma_start(
                        dbg["d_qwbf"][:, 0:Wd], qwbf[:, 0:Wd]),
                        wait=[(ev, qv)], inc=st_c)
                    S.op("sync", lambda g, Wd=Wd: g.dma_start(
                        dbg["d_qwin"][:, 0:Wd], qwtile[:, 0:Wd]),
                        wait=[(ev, qv)], inc=st_c)
                for c in range(Wd // 128):
                    ti = c % 2
                    pv = S.op("tensor", lambda t, c=c: t.transpose(
                        ctx.p_tb[:, 0:128], qwbf[:, 128 * c:128 * (c + 1)],
                        ctx.ident[:]),
                        wait=([(ev, qv)] if c == 0 else []) +
                        ctx.guard("p_tb"), inc=pe)
                    ec = S.op("vector", lambda v, ti=ti: v.tensor_copy(
                        ttile[:, ti, :], ctx.p_tb[:, 0:128]),
                        wait=[(pe, pv)] + tw[ti], inc=ev)
                    ctx.set_guard("p_tb", ev, ec)
                    sv = S.op("sync", lambda g, nt=nt, c=c, ti=ti, scr=scr,
                              Wd=Wd: g.dma_start(
                                  AP(scr, (128 * c) * ntiles * 128 + 128 * nt,
                                     [[ntiles * 128, 128], [1, 128]]),
                                  ttile[:, ti, :]), wait=[(ev, ec)], inc=st_c)
                    tw[ti] = [(st_c, sv)]
                qw_war = qw_war + [(pe, pv)]
            qw_state["war"] = qw_war

        quant_w(wg_d, wqg_scr, I3 // 128, D)
        if DBG:
            S.barrier()
            S.op("gpsimd", lambda g: g.dma_start(dbg["d_qwin"].ap(),
                 AP(wqg_scr, 0, [[2816, 128], [1, 2816]])), inc=st_c)
            S.barrier()
        quant_w(wu_d, wqu_scr, I3 // 128, D)
        quant_w(wd_d, wqd_scr, D // 128, I3)
        S.barrier()
        if DBG:
            S.op("sync", lambda g: g.dma_start(dbg["d_wqg"].ap(),
                                               wqg_scr.ap()), inc=st_c)
            S.op("sync", lambda g: g.dma_start(dbg["d_wqd"].ap(),
                                               wqd_scr.ap()), inc=st_c)
            S.barrier()

        sw = [0, 0]
        res_war = [[], []]
        rt_war = []
        for nh in range(2):
            wwar = [(pe, pe.n)] if nh > 0 else []
            for cc in range(NC):
                l1_ = S.op("gpsimd", lambda g, nh=nh, cc=cc: g.dma_start(
                    wgh[:, cc, :], AP(wqg_scr, cc * 128 * I3 + IH * nh,
                                      [[I3, 128], [1, IH]])),
                    wait=(wwar if cc == 0 else ()), inc=ld)
            for cc in range(NC):
                l2_ = S.op("gpsimd", lambda g, nh=nh, cc=cc: g.dma_start(
                    wuh[:, cc, :], AP(wqu_scr, cc * 128 * I3 + IH * nh,
                                      [[I3, 128], [1, IH]])), inc=ld)
            for cc in range(11):
                l3_ = S.op("gpsimd", lambda g, nh=nh, cc=cc: g.dma_start(
                    wdh[:, cc, :], AP(wqd_scr, (IH * nh + 128 * cc) * D,
                                      [[D, 128], [1, D]])), inc=ld)
            for tb in range(NT):
                gv = None
                for fb in range(3):
                    f0 = 512 * fb
                    f1 = min(512 * (fb + 1), IH)
                    pairs = [(xqT[:, cc, 128 * tb:128 * (tb + 1)],
                              wgh[:, cc, f0:f1]) for cc in range(NC)]
                    gv = mm_chain(S, ctx, "p_g", p_g[:, f0:f1], pairs,
                                  wait=[(ld, l1_)] if fb == 0 else ())
                ge = S.op("scalar", lambda sc: sc.activation(
                    gate_sb[:, 0:IH], p_g[:, 0:IH], AF.Silu),
                    wait=[(pe, gv)], inc=act)
                ctx.set_guard("p_g", act, ge)
                uv = None
                for fb in range(3):
                    f0 = 512 * fb
                    f1 = min(512 * (fb + 1), IH)
                    pairs = [(xqT[:, cc, 128 * tb:128 * (tb + 1)],
                              wuh[:, cc, f0:f1]) for cc in range(NC)]
                    uv = mm_chain(S, ctx, "p_g", p_g[:, f0:f1], pairs,
                                  wait=[(ld, l2_)] if fb == 0 else ())
                he = S.op("vector", lambda v: v.tensor_tensor(
                    h_sb[:, 0:IH], gate_sb[:, 0:IH], p_g[:, 0:IH], ALU.mult),
                    wait=[(pe, uv), (act, ge)], inc=ev)
                ctx.set_guard("p_g", ev, he)
                qv = None
                for hf in range(3):
                    f0 = 512 * hf
                    f1 = min(512 * (hf + 1), IH)
                    qv = emit_quant(S, ctx, h_sb[:, f0:f1], h_sb[:, f0:f1],
                                    f1 - f0, qt, qacc, qsgn, qsc,
                                    wait=[(ev, he)] if hf == 0 else ())
                for c in range(11):
                    pv = S.op("tensor", lambda t, c=c: t.transpose(
                        ctx.p_tb[:, 0:128], h_sb[:, 128 * c:128 * (c + 1)],
                        ctx.ident[:]),
                        wait=([(ev, qv)] if c == 0 else []) +
                        ctx.guard("p_tb"), inc=pe)
                    tv = S.op("vector", lambda v, c=c: v.tensor_copy(
                        hqT[:, c, :], ctx.p_tb[:, 0:128]),
                        wait=[(pe, pv)], inc=ev)
                    ctx.set_guard("p_tb", ev, tv)
                dv = None
                for fb in range(2):
                    pairs = [(hqT[:, cc, :],
                              wdh[:, cc, 512 * fb:512 * (fb + 1)])
                             for cc in range(11)]
                    dv = mm_chain(S, ctx, "p_b", p_b[:, 512 * fb:512 * (fb + 1)],
                                  pairs,
                                  wait=[(ld, l3_), (ev, tv)] if fb == 0 else ())
                if nh == 0:
                    ae = S.op("scalar", lambda sc, tb=tb: sc.activation(
                        acc2[:, tb, :], p_b[:, 0:1024], AF.Identity),
                        wait=[(pe, dv)], inc=act)
                    ctx.set_guard("p_b", act, ae)
                else:
                    bi = tb % 2
                    lvx = S.op("gpsimd", lambda g, tb=tb, bi=bi: g.dma_start(
                        ctx.ln_xt[:, bi, :],
                        xb_d.ap().rearrange("(n p) d -> n p d", p=128)[tb]),
                        wait=res_war[bi], inc=ld)
                    lvs = S.op("gpsimd", lambda g, tb=tb, bi=bi: g.dma_start(
                        stile[:, bi, :],
                        s_scr.ap().rearrange("(n p) d -> n p d", p=128)[tb]),
                        inc=ld)
                    fv = chain(S, "vector", [
                        lambda v, tb=tb: v.tensor_tensor(
                            rtmp[:, 0, :], p_b[:, 0:1024], acc2[:, tb, :],
                            ALU.add),
                    ], wait=[(pe, dv)])
                    ctx.set_guard("p_b", ev, fv)
                    av = S.op("scalar", lambda sc: sc.activation(
                        rtmp[:, 1, :], rtmp[:, 0, :], AF.Silu),
                        wait=[(ev, fv)] + rt_war, inc=act)
                    e = chain(S, "vector", [
                        lambda v, bi=bi: v.tensor_tensor(
                            rtmp[:, 1, :], rtmp[:, 1, :], stile[:, bi, :],
                            ALU.mult),
                        lambda v, bi=bi: v.tensor_tensor(
                            rtmp[:, 1, :], rtmp[:, 1, :], ctx.ln_xt[:, bi, :],
                            ALU.add),
                    ], wait=[(act, av), (ld, lvs)])
                    sv = S.op("sync", lambda g, tb=tb: g.dma_start(
                        x2_d.ap().rearrange("(n p) d -> n p d", p=128)[tb],
                        rtmp[:, 1, :]), wait=[(ev, e)], inc=st_c)
                    res_war[bi] = [(st_c, sv)]
                    rt_war = [(st_c, sv)]
            sw[nh] = pe.n
            if DBG and nh == 0:
                S.barrier()
                S.op("sync", lambda g: g.dma_start(dbg["d_acc2"].ap(),
                     acc2.ap().rearrange("p a b -> p (a b)")), inc=st_c)
                S.barrier()
        S.barrier()
        block = es.enter_context(nc.Block())
        S.emit(block)
    return nc




def build_L2():
    nc = bass.Bass("TRN2", target_bir_lowering=False)

    def din(name, shape, dt=F32):
        return nc.dram_tensor(name, shape, dt, kind="ExternalInput")

    xb_d = din("xb", [T, D])
    w_ns = [din("wlh", [D, D]), din("wlhT", [D, D]),
            din("wrh", [D, D]), din("wrhT", [D, D])]
    wq_d = din("wq", [D, D])
    wvT_d = din("wvT", [D, D])
    wcT_d = din("wcT", [D, D])
    idv_d = din("idv", [128, 8])
    w1_d = din("w1", [D, 4096])
    b1_d = din("b1", [1, 32 * 128])
    w2_d = din("w2", [4096, D])
    b2_d = din("b2", [1, D])
    x3_d = nc.dram_tensor("x3", [T, D], F32, kind="ExternalOutput")
    s_scr = nc.dram_tensor("s_scr", [T, D], BF16, kind="Internal")
    qt_scr = nc.dram_tensor("qt_scr", [T, D], F32, kind="Internal")
    import os as _os
    DBG = _os.environ.get("KDBG", "") == "1"
    dbg = {}
    if DBG:
        dbg = {
            "d_ylh": nc.dram_tensor("d_ylh", [128, NC * D], BF16,
                                    kind="ExternalOutput"),
            "d_qt": nc.dram_tensor("d_qt", [T, D], F32, kind="ExternalOutput"),
            "d_qs": nc.dram_tensor("d_qs", [128, NC * T], BF16,
                                   kind="ExternalOutput"),
            "d_s": nc.dram_tensor("d_s", [T, D], BF16, kind="ExternalOutput"),
        }

    es = contextlib.ExitStack()
    with es:
        ctx = Ctx(nc, es)
        S = Sched(nc, es)
        S.C = dict(ld=S.ctr("ld", 16), st=S.ctr("st", 16), pe=S.ctr("pe", 1),
                   ev=S.ctr("ev", 1), act=S.ctr("act", 1), pool=S.ctr("pool", 1))
        C = S.C
        ld, st_c, pe, ev, act, pool = (C["ld"], C["st"], C["pe"], C["ev"],
                                       C["act"], C["pool"])

        Xb = [ctx.sb_at(f"Xb{i}", [128, NC, D], BF16, 56 * KB + 16 * KB * i)
              for i in range(2)]
        vT = ctx.sb_at("vT", [128, NC, T], BF16, 56 * KB)
        gacc = ctx.sb_at("gacc", [128, NT, D], BF16, 56 * KB)
        Asb = ctx.sb_at("Asb", [128, NC, D], BF16, 88 * KB)
        M2sb = ctx.sb_at("M2sb", [128, NC, D], BF16, 88 * KB)
        qsT = ctx.sb_at("qsT", [128, NC, T], BF16, 88 * KB)
        hT = ctx.sb_at("hT", [128, 4, 512], BF16, 88 * KB)
        mtmp = ctx.sb_at("mtmp", [128, 2, D], BF16, 92 * KB)
        rtmp = ctx.sb_at("rtmp", [128, 2, D], F32, 96 * KB)
        Yb = [ctx.sb_at(f"Yb{i}", [128, NC, D], BF16, 104 * KB + 16 * KB * i)
              for i in range(2)]
        w1e = [ctx.sb_at(f"w1e{i}", [128, NC, 512], BF16, 120 * KB + 8 * KB * i)
               for i in range(2)]
        ylh = ctx.sb_at("ylh", [128, NC, D], BF16, 136 * KB)
        w2e = [ctx.sb_at(f"w2e{i}", [128, 4, D], BF16, 136 * KB + 8 * KB * i)
               for i in range(2)]
        wq_sb = ctx.sb_at("wq_sb", [128, NC, D], BF16, 152 * KB)
        wcT_sb = ctx.sb_at("wcT_sb", [128, NC, D], BF16, 152 * KB)
        srows = ctx.sb_at("srows", [1, 4, D], F32, 152 * KB)
        hrow = ctx.sb_at("hrow", [1, D], BF16, 168 * KB)
        xnT = ctx.sb_at("xnT", [128, NC, T], BF16, 170 * KB)
        qstage = ctx.sb_at("qstage", [128, D], F32, 202 * KB)

        ctx.ln_xt = ctx.sb("ln_xt", [128, 2, D], F32)
        ctx.ln_xnb = ctx.sb("ln_xnb", [128, 2, D], BF16)
        ctx.ln_st = ctx.sb("ln_st", [128, 8], F32)
        ctx.ident = ctx.sb("ident", [128, 128], BF16)
        ctx.identf = ctx.sb("identf", [128, 128], F32)
        ctx.onescol = ctx.sb("onescol", [1, 128], BF16)
        ctx.ones_bcol = ctx.sb("ones_bcol", [128, 1], BF16)
        ctx.scr128 = ctx.sb("scr128", [128, 128], F32)
        ctx.actdump = rtmp
        stile = ctx.sb("stile", [128, 2, D], BF16)
        b1c = ctx.sb("b1c", [128, 32], F32)
        b2row = ctx.sb("b2row", [1, D], BF16)
        hcol0 = ctx.sb("hcol0", [128, 8], BF16)
        nsst = ctx.sb("nsst", [128, 12], F32)
        smallb2 = ctx.sb("smallb2", [1, 4], BF16)
        nsstb = ctx.sb("nsstb", [128, 1], BF16)
        nsrow = ctx.sb("nsrow", [1, 8], F32)
        whalf = ctx.sb("whalf", [128, 4, D], F32)

        p_a = ctx.ps("p_a", [128, 1024], F32)
        p_b = ctx.ps("p_b", [128, 1024], F32)
        ctx.p_c = ctx.ps("p_c", [128, 512], F32)
        p_c = ctx.p_c
        ctx.p_tb = ctx.ps("p_tb", [128, 512], BF16)
        p_tb = ctx.p_tb

        emit_const_gen(S, ctx)
        S.op("gpsimd", lambda g: g.dma_start(hcol0[:], idv_d.ap()), inc=ld)
        S.op("gpsimd", lambda g: g.dma_start(
            b1c.ap(), AP(b1_d, 0, [[32, 128], [1, 32]])), inc=ld)
        S.op("gpsimd", lambda g: g.dma_start(b2row[:], b2_d.ap()), inc=ld)
        S.barrier()

        emit_ln(S, ctx, xb_d, xnT)
        S.barrier()

        for mi in range(2):
            w_raw, w_t = w_ns[2 * mi], w_ns[2 * mi + 1]
            sv = None
            for hf in range(2):
                lv = S.op("gpsimd", lambda g, hf=hf, w_raw=w_raw: g.dma_start(
                    whalf.ap(), AP(w_raw, hf * 4 * 128 * D,
                                   [[D, 128], [128 * D, 4], [1, D]])),
                    wait=[(ev, ev.n), (act, act.n)], inc=ld)
                for a_ in range(4):
                    sv = S.op("scalar", lambda sc, hf=hf, a_=a_: sc.activation(
                        ctx.actdump[:, 0, :], whalf[:, a_, :], AF.Square,
                        accum_out=nsst[:, 4 * hf + a_:4 * hf + a_ + 1]),
                        wait=([(ld, lv)] if a_ == 0 else ()), inc=act)

            def sum8(v):
                v.tensor_tensor(nsst[:, 8:9], nsst[:, 0:1], nsst[:, 1:2],
                                ALU.add)
                for k_ in range(2, 8):
                    v.tensor_tensor(nsst[:, 8:9], nsst[:, 8:9],
                                    nsst[:, k_:k_ + 1], ALU.add)
                return v.tensor_copy(nsst[:, 2:3], nsst[:, 8:9])

            e1 = chain(S, "vector", [
                lambda v: v.tensor_tensor(nsst[:, 8:9], nsst[:, 0:1],
                                          nsst[:, 1:2], ALU.add),
                lambda v: v.tensor_tensor(nsst[:, 8:9], nsst[:, 8:9],
                                          nsst[:, 2:3], ALU.add),
                lambda v: v.tensor_tensor(nsst[:, 8:9], nsst[:, 8:9],
                                          nsst[:, 3:4], ALU.add),
                lambda v: v.tensor_tensor(nsst[:, 8:9], nsst[:, 8:9],
                                          nsst[:, 4:5], ALU.add),
                lambda v: v.tensor_tensor(nsst[:, 8:9], nsst[:, 8:9],
                                          nsst[:, 5:6], ALU.add),
                lambda v: v.tensor_tensor(nsst[:, 8:9], nsst[:, 8:9],
                                          nsst[:, 6:7], ALU.add),
                lambda v: v.tensor_tensor(nsst[:, 8:9], nsst[:, 8:9],
                                          nsst[:, 7:8], ALU.add),
                lambda v: v.tensor_copy(nsst[:, 2:3], nsst[:, 8:9]),
            ], wait=[(act, sv)])
            cb = S.op("vector", lambda v: v.tensor_copy(smallb2[0:1, 2:3]
                      if False else hcol0[:, 0:1] if False else
                      ctx.ones_bcol[:] if False else nsstb[:], nsst[:, 2:3]),
                      wait=[(ev, e1)], inc=ev)
            pv = S.op("tensor", lambda t: t.matmul(
                p_c[0:1, 0:1], ctx.ones_bcol[:], nsstb[:], start=True,
                stop=True), wait=[(ev, cb)] + ctx.guard("p_c"), inc=pe)
            e2 = S.op("vector", lambda v: v.tensor_copy(nsrow[:, 0:1],
                                                        p_c[0:1, 0:1]),
                      wait=[(pe, pv)], inc=ev)
            ctx.set_guard("p_c", ev, e2)
            a1 = S.op("scalar", lambda sc: sc.sqrt(nsrow[:, 1:2], nsrow[:, 0:1]),
                      wait=[(ev, e2)], inc=act)
            e3 = chain(S, "vector", [
                lambda v: v.tensor_scalar(nsrow[:, 1:2], nsrow[:, 1:2], 1e-8,
                                          None, ALU.add),
                lambda v: v.reciprocal(nsrow[:, 2:3], nsrow[:, 1:2]),
            ], wait=[(act, a1)])
            cv = emit_bcast_col(S, ctx, nsst[:, 3:4], nsrow[:, 2:3],
                                smallb2[:, 0:1], wait=[(ev, e3)])
            for (dst, srcd) in ((Xb[0], w_raw), (Yb[0], w_t)):
                for hf in range(2):
                    lv = S.op("gpsimd", lambda g, hf=hf, srcd=srcd: g.dma_start(
                        whalf.ap(), AP(srcd, hf * 4 * 128 * D,
                                       [[D, 128], [128 * D, 4], [1, D]])),
                        wait=[(ev, ev.n)], inc=ld)
                    S.op("vector", lambda v, hf=hf, dst=dst: v.tensor_scalar(
                        dst[:, 4 * hf:4 * (hf + 1), :],
                        whalf.ap(), nsst[:, 3:4], None, ALU.mult),
                        wait=[(ld, lv), (ev, cv)], inc=ev)
            S.barrier()
            for it in range(12):
                pi = it % 2
                po = 1 - pi
                for ic in range(NC):
                    for jf in range(2):
                        pairs = [(Yb[pi][:, cc, 128 * ic:128 * (ic + 1)],
                                  Yb[pi][:, cc, 512 * jf:512 * (jf + 1)])
                                 for cc in range(NC)]
                        v = mm_chain(S, ctx, "p_b", p_b[:, 512 * jf:512 * (jf + 1)],
                                     pairs)
                        e = S.op("scalar", lambda sc, ic=ic, jf=jf: sc.activation(
                            Asb[:, ic, 512 * jf:512 * (jf + 1)],
                            p_b[:, 512 * jf:512 * (jf + 1)], AF.Identity,
                            scale=-0.5), wait=[(pe, v)], inc=act)
                        ctx.set_guard("p_b", act, e)
                aev = act.n
                for ic in range(NC):
                    for jf in range(2):
                        pairs = [(Asb[:, cc, 128 * ic:128 * (ic + 1)],
                                  Xb[pi][:, cc, 512 * jf:512 * (jf + 1)])
                                 for cc in range(NC)]
                        v = mm_chain(S, ctx, "p_b", p_b[:, 512 * jf:512 * (jf + 1)],
                                     pairs, wait=[(act, aev)])
                        e = S.op("vector", lambda vv, ic=ic, jf=jf, pi=pi, po=po:
                                 vv.scalar_tensor_tensor(
                                     Xb[po][:, ic, 512 * jf:512 * (jf + 1)],
                                     Xb[pi][:, ic, 512 * jf:512 * (jf + 1)],
                                     1.5, p_b[:, 512 * jf:512 * (jf + 1)],
                                     ALU.mult, ALU.add), wait=[(pe, v)], inc=ev)
                        ctx.set_guard("p_b", ev, e)
                for ic in range(NC):
                    for jf in range(2):
                        pairs = [(Xb[pi][:, cc, 128 * ic:128 * (ic + 1)],
                                  Asb[:, cc, 512 * jf:512 * (jf + 1)])
                                 for cc in range(NC)]
                        v = mm_chain(S, ctx, "p_b", p_b[:, 512 * jf:512 * (jf + 1)],
                                     pairs)
                        e = S.op("vector", lambda vv, ic=ic, jf=jf, pi=pi, po=po:
                                 vv.scalar_tensor_tensor(
                                     Yb[po][:, ic, 512 * jf:512 * (jf + 1)],
                                     Yb[pi][:, ic, 512 * jf:512 * (jf + 1)],
                                     1.5, p_b[:, 512 * jf:512 * (jf + 1)],
                                     ALU.mult, ALU.add), wait=[(pe, v)], inc=ev)
                        ctx.set_guard("p_b", ev, e)
                S.barrier()
            if mi == 0:
                S.op("vector", lambda v: v.tensor_copy(
                    ylh.ap().rearrange("p a b -> p (a b)"),
                    Yb[0].ap().rearrange("p a b -> p (a b)")), inc=ev)
                S.barrier()
        if DBG:
            S.op("sync", lambda g: g.dma_start(
                dbg["d_ylh"].ap(), ylh.ap().rearrange("p a b -> p (a b)")),
                inc=st_c)
            S.barrier()

        for cc in range(NC):
            S.op("gpsimd", lambda g, cc=cc: g.dma_start(
                wq_sb[:, cc, :],
                wq_d.ap().rearrange("(c p) d -> c p d", p=128)[cc]), inc=ld)
        S.barrier()
        for ic in range(NC):
            for jf in range(2):
                pairs = [(wq_sb[:, cc, 128 * ic:128 * (ic + 1)],
                          Yb[0][:, cc, 512 * jf:512 * (jf + 1)])
                         for cc in range(NC)]
                v = mm_chain(S, ctx, "p_b", p_b[:, 512 * jf:512 * (jf + 1)],
                             pairs)
                e = S.op("scalar", lambda sc, ic=ic, jf=jf: sc.activation(
                    M2sb[:, ic, 512 * jf:512 * (jf + 1)],
                    p_b[:, 512 * jf:512 * (jf + 1)], AF.Identity),
                    wait=[(pe, v)], inc=act)
                ctx.set_guard("p_b", act, e)
        S.barrier()
        qs_war = []
        for tb in range(NT):
            for fb in range(2):
                pairs = [(xnT[:, cc, 128 * tb:128 * (tb + 1)],
                          M2sb[:, cc, 512 * fb:512 * (fb + 1)])
                         for cc in range(NC)]
                v = mm_chain(S, ctx, "p_b", p_b[:, 512 * fb:512 * (fb + 1)],
                             pairs)
            e = S.op("vector", lambda vv, tb=tb: vv.tensor_copy(
                qstage[:], p_b[:, 0:1024]), wait=[(pe, v)] + qs_war, inc=ev)
            ctx.set_guard("p_b", ev, e)
            sv = S.op("sync", lambda g, tb=tb: g.dma_start(
                qt_scr.ap().rearrange("(n p) d -> n p d", p=128)[tb],
                qstage[:]), wait=[(ev, e)], inc=st_c)
            qs_war = [(st_c, sv)]
        for cc in range(NC):
            S.op("gpsimd", lambda g, cc=cc: g.dma_start(
                wq_sb[:, cc, :],
                wvT_d.ap().rearrange("(c p) d -> c p d", p=128)[cc]),
                wait=[(pe, pe.n)], inc=ld)
        S.barrier()
        emit_proj_T(S, ctx, wq_sb, xnT, vT)
        S.barrier()
        if DBG:
            S.op("sync", lambda g: g.dma_start(dbg["d_qt"].ap(), qt_scr.ap()),
                 inc=st_c)
            S.barrier()

        qrow = srows[:, 0:2, :]
        grow = srows[:, 2:3, :]
        drow = srows[:, 3:4, :]
        l0 = S.op("sync", lambda g: g.dma_start(
            qrow[0:1, 0, :], AP(qt_scr, 0, [[D, 1], [1, D]])), inc=st_c)
        qld = [l0, 0]
        scn = ctx.sb("scn", [1, 8], F32)
        prev_ev = 0
        for t in range(T):
            bi = t % 2
            if t + 1 < T:
                qld[(t + 1) % 2] = S.op("sync", lambda g, t=t: g.dma_start(
                    qrow[0:1, (t + 1) % 2, :],
                    AP(qt_scr, (t + 1) * D, [[D, 1], [1, D]])),
                    wait=[(ev, prev_ev)], inc=st_c)
            v = None
            for cc in range(NC):
                lcol = (hcol0[:, cc:cc + 1] if t == 0
                        else qsT[:, cc, t - 1:t])
                for fb in range(2):
                    v = S.op("tensor", lambda tt, lcol=lcol, cc=cc, fb=fb:
                             tt.matmul(p_a[0:1, 512 * fb:512 * (fb + 1)],
                                       lcol, ylh[:, cc, 512 * fb:512 * (fb + 1)],
                                       start=(cc == 0), stop=(cc == NC - 1)),
                             wait=(ctx.guard("p_a") + [(ev, prev_ev)]
                                   if cc == 0 and fb == 0 else ()), inc=pe)
            e1 = S.op("vector", lambda vv, bi=bi: vv.tensor_tensor(
                grow[0:1, 0, :], p_a[0:1, 0:1024], qrow[0:1, bi, :], ALU.add),
                wait=[(pe, v), (st_c, qld[bi])], inc=ev)
            ctx.set_guard("p_a", ev, e1)
            a1 = S.op("scalar", lambda sc: sc.activation(
                drow[0:1, 0, :], grow[0:1, 0, :], AF.Square,
                accum_out=scn[:, 0:1]), wait=[(ev, e1)], inc=act)
            e2 = chain(S, "vector", [
                lambda vv: vv.tensor_scalar(scn[:, 1:2], scn[:, 0:1],
                                            1.0 / D, 1e-6, ALU.mult, ALU.add),
                lambda vv: vv.reciprocal(scn[:, 2:3], scn[:, 1:2]),
            ], wait=[(act, a1)])
            a2 = S.op("scalar", lambda sc: sc.sqrt(scn[:, 3:4], scn[:, 2:3]),
                      wait=[(ev, e2)], inc=act)
            e3 = S.op("vector", lambda vv: vv.tensor_scalar(
                hrow[0:1, :], grow[0:1, 0, :], scn[:, 3:4], None, ALU.mult),
                wait=[(act, a2)], inc=ev)
            pv = None
            for cc in range(NC):
                pv = S.op("tensor", lambda tt, cc=cc: tt.transpose(
                    p_tb[:, 2 * cc:2 * cc + 1],
                    hrow[0:1, 128 * cc:128 * (cc + 1)],
                    ctx.ident[0:1, 0:1]),
                    wait=([(ev, e3)] + ctx.guard("p_tb") if cc == 0 else ()),
                    inc=pe)
            prev_ev = S.op("vector", lambda vv, t=t: vv.tensor_copy(
                qsT.ap()[:, :, t:t + 1], p_tb[:, 0:16].rearrange(
                    "p (a b) -> p a b", b=2)[:, :, 0:1]),
                wait=[(pe, pv)], inc=ev)
            ctx.set_guard("p_tb", ev, prev_ev)
        S.barrier()
        if DBG:
            S.op("sync", lambda g: g.dma_start(
                dbg["d_qs"].ap(), qsT.ap().rearrange("p a b -> p (a b)")),
                inc=st_c)
            S.barrier()

        for cc in range(NC):
            S.op("gpsimd", lambda g, cc=cc: g.dma_start(
                wcT_sb[:, cc, :],
                wcT_d.ap().rearrange("(c p) d -> c p d", p=128)[cc]), inc=ld)
        S.op("vector", lambda v: v.tensor_tensor(
            qsT.ap().rearrange("p a b -> p (a b)"),
            qsT.ap().rearrange("p a b -> p (a b)"),
            vT.ap().rearrange("p a b -> p (a b)"), ALU.mult), inc=ev)
        S.barrier()
        sw_war = [[], []]
        for tb in range(NT):
            si = tb % 2
            for fb in range(2):
                pairs = [(qsT[:, cc, 128 * tb:128 * (tb + 1)],
                          wcT_sb[:, cc, 512 * fb:512 * (fb + 1)])
                         for cc in range(NC)]
                v = mm_chain(S, ctx, "p_b", p_b[:, 512 * fb:512 * (fb + 1)],
                             pairs)
            e = S.op("scalar", lambda sc, si=si: sc.activation(
                stile[:, si, :], p_b[:, 0:1024], AF.Identity),
                wait=[(pe, v)] + sw_war[si], inc=act)
            ctx.set_guard("p_b", act, e)
            sv = S.op("sync", lambda g, tb=tb, si=si: g.dma_start(
                s_scr.ap().rearrange("(n p) d -> n p d", p=128)[tb],
                stile[:, si, :]), wait=[(act, e)], inc=st_c)
            sw_war[si] = [(st_c, sv)]
        S.barrier()
        if DBG:
            S.op("sync", lambda g: g.dma_start(dbg["d_s"].ap(), s_scr.ap()),
                 inc=st_c)
            S.barrier()

        wload = [0, 0]
        wfree = [0, 0]

        def load_we(q8):
            par = q8 % 2
            for cc in range(NC):
                S.op("gpsimd", lambda g, q8=q8, par=par, cc=cc: g.dma_start(
                    w1e[par][:, cc, :],
                    AP(w1_d, cc * 128 * 4096 + 512 * q8,
                       [[4096, 128], [1, 512]])),
                    wait=([(pe, wfree[par])] if cc == 0 else ()), inc=ld)
            lv = None
            for cc in range(4):
                lv = S.op("gpsimd", lambda g, q8=q8, par=par, cc=cc:
                          g.dma_start(w2e[par][:, cc, :],
                                      AP(w2_d, (512 * q8 + 128 * cc) * D,
                                         [[D, 128], [1, D]])), inc=ld)
            wload[par] = lv

        load_we(0)
        mt_war = [0, 0]
        hT_war = 0
        for q8 in range(8):
            par = q8 % 2
            if q8 + 1 < 8:
                load_we(q8 + 1)
            for TB in range(4):
                t0 = 512 * TB
                for hc in range(4):
                    pairs = [(w1e[par][:, cc, 128 * hc:128 * (hc + 1)],
                              xnT[:, cc, t0:t0 + 512]) for cc in range(NC)]
                    v = mm_chain(S, ctx, "p_c", p_c[:, 0:512], pairs,
                                 wait=[(ld, wload[par])])
                    gval = S.op("scalar", lambda sc, hc=hc, q8=q8: sc.activation(
                        hT[:, hc, :], p_c[:, 0:512], AF.Relu,
                        bias=b1c[:, 4 * q8 + hc:4 * q8 + hc + 1]),
                        wait=[(pe, v), (pe, hT_war)], inc=act)
                    ctx.set_guard("p_c", act, gval)
                for tj in range(4):
                    tb = 4 * TB + tj
                    gb = ctx.guard("p_b")
                    vv = None
                    for hc in range(4):
                        for fb in range(2):
                            isf = (hc == 0)
                            isl = (hc == 3 and q8 != 0)
                            w = ([(act, gval)] + gb
                                 if hc == 0 and fb == 0 else ())
                            vv = S.op("tensor",
                                      lambda t_, tj=tj, hc=hc, fb=fb, par=par,
                                      isf=isf, isl=isl: t_.matmul(
                                          p_b[:, 512 * fb:512 * (fb + 1)],
                                          hT[:, hc, 128 * tj:128 * (tj + 1)],
                                          w2e[par][:, hc,
                                                   512 * fb:512 * (fb + 1)],
                                          start=isf, stop=isl),
                                      wait=w, inc=pe)
                    if q8 == 0:
                        for fb in range(2):
                            vv = S.op("tensor", lambda t_, fb=fb: t_.matmul(
                                p_b[:, 512 * fb:512 * (fb + 1)],
                                ctx.onescol[:],
                                b2row[:, 512 * fb:512 * (fb + 1)],
                                start=False, stop=True), inc=pe)
                    mi = tb % 2
                    if q8 == 0:
                        eva = S.op("scalar", lambda sc, tb=tb: sc.activation(
                            gacc[:, tb, :], p_b[:, 0:1024], AF.Identity),
                            wait=[(pe, vv)], inc=act)
                        ctx.set_guard("p_b", act, eva)
                    else:
                        eva = S.op("scalar", lambda sc, mi=mi: sc.activation(
                            mtmp[:, mi, :], p_b[:, 0:1024], AF.Identity),
                            wait=[(pe, vv), (pool, mt_war[mi])], inc=act)
                        ctx.set_guard("p_b", act, eva)
                        mt_war[mi] = S.op("gpsimd", lambda g, mi=mi, tb=tb:
                                          g.tensor_add(gacc[:, tb, :],
                                                       gacc[:, tb, :],
                                                       mtmp[:, mi, :]),
                                          wait=[(act, eva)], inc=pool)
                hT_war = S.C["pe"].n
            wfree[par] = S.C["pe"].n
        S.barrier()

        xr = xb_d.ap().rearrange("(n p) d -> n p d", p=128)
        sr = s_scr.ap().rearrange("(n p) d -> n p d", p=128)
        x3r = x3_d.ap().rearrange("(n p) d -> n p d", p=128)
        res_war = [[], []]
        for i in range(NT):
            bi = i % 2
            lvx = S.op("gpsimd", lambda g, i=i, bi=bi: g.dma_start(
                ctx.ln_xt[:, bi, :], xr[i]), wait=res_war[bi], inc=ld)
            lvs = S.op("gpsimd", lambda g, i=i, bi=bi: g.dma_start(
                ctx.ln_xnb[:, bi, :], sr[i]), inc=ld)
            a = S.op("scalar", lambda sc, i=i, bi=bi: sc.activation(
                rtmp[:, bi, :], gacc[:, i, :], AF.Silu),
                wait=[(ld, lvs)] + res_war[bi], inc=act)
            e = chain(S, "vector", [
                lambda v, bi=bi: v.tensor_tensor(rtmp[:, bi, :],
                                                 rtmp[:, bi, :],
                                                 ctx.ln_xnb[:, bi, :],
                                                 ALU.mult),
                lambda v, bi=bi: v.tensor_tensor(rtmp[:, bi, :],
                                                 rtmp[:, bi, :],
                                                 ctx.ln_xt[:, bi, :],
                                                 ALU.add),
            ], wait=[(act, a), (ld, lvx)])
            sv = S.op("sync", lambda g, i=i, bi=bi: g.dma_start(
                x3r[i], rtmp[:, bi, :]), wait=[(ev, e)], inc=st_c)
            res_war[bi] = [(st_c, sv)]
        S.barrier()
        block = es.enter_context(nc.Block())
        S.emit(block)
    return nc





NH = 16
DK = 64
CCAP = 1024


def build_L3():
    nc = bass.Bass("TRN2", target_bir_lowering=False)

    def din(name, shape, dt=F32):
        return nc.dram_tensor(name, shape, dt, kind="ExternalInput")

    xb_d = din("xb", [T, D])
    wqT_d = din("wqT", [D, D])
    wkT_d = din("wkT", [D, DK])
    wvT_d = din("wvT", [D, DK])
    woT_d = din("woT", [D, D])
    scale_d = din("scale", [1, 1])
    wgc_d = din("wgc", [128, NC])
    wqf_d = din("wqf", [D, D])
    wkf_d = din("wkf", [D, D])
    wvf_d = din("wvf", [D, D])
    wof_d = din("wof", [D, D])
    x4_d = nc.dram_tensor("x4", [T, D], F32, kind="ExternalOutput")
    s_scr = nc.dram_tensor("s_scr", [T, D], BF16, kind="Internal")
    import os as _os
    DBG = _os.environ.get("KDBG", "") == "1"
    if DBG:
        dbg = {
            "d_qpT": nc.dram_tensor("d_qpT", [128, NC * T], BF16,
                                    kind="ExternalOutput"),
            "d_kT": nc.dram_tensor("d_kT", [128, T], BF16,
                                   kind="ExternalOutput"),
            "d_vtm": nc.dram_tensor("d_vtm", [128, NT * DK], BF16,
                                    kind="ExternalOutput"),
            "d_stats": nc.dram_tensor("d_stats", [128, 576], F32,
                                      kind="ExternalOutput"),
            "d_stats2": nc.dram_tensor("d_stats2", [128, 576], F32,
                                       kind="ExternalOutput"),
            "d_s": nc.dram_tensor("d_s", [T, D], BF16,
                                  kind="ExternalOutput"),
            "d_gate": nc.dram_tensor("d_gate", [128, 64], F32,
                                     kind="ExternalOutput"),
            "d_fq": nc.dram_tensor("d_fq", [128, NC * T], BF16,
                                   kind="ExternalOutput"),
            "d_fk": nc.dram_tensor("d_fk", [128, NC * T], BF16,
                                   kind="ExternalOutput"),
            "d_fv": nc.dram_tensor("d_fv", [128, NT * D], BF16,
                                   kind="ExternalOutput"),
        }

    M0 = 0
    F0 = 256
    LQ = 512
    LI = 528
    NC0 = 544
    SC = 560
    SLD = 561
    NM = 562
    CH = 563

    es = contextlib.ExitStack()
    with es:
        ctx = Ctx(nc, es)
        S = Sched(nc, es)
        S.C = dict(ld=S.ctr("ld", 16), st=S.ctr("st", 16), pe=S.ctr("pe", 1),
                   ev=S.ctr("ev", 1), act=S.ctr("act", 1), pool=S.ctr("pool", 1))
        C = S.C
        ld, st_c, pe, ev, act, pool = (C["ld"], C["st"], C["pe"], C["ev"],
                                       C["act"], C["pool"])

        wk_sb = ctx.sb_at("wk_sb", [128, NC, DK], BF16, 52 * KB)
        wv_sb = ctx.sb_at("wv_sb", [128, NC, DK], BF16, 53 * KB)
        ao4 = ctx.sb_at("ao4", [128, 4, D], BF16, 54 * KB)
        xnT = ctx.sb_at("xnT", [128, NC, T], BF16, 64 * KB)
        qpT = ctx.sb_at("qpT", [128, NC, T], BF16, 96 * KB)
        fqpT = ctx.sb_at("fqpT", [128, NC, T], BF16, 96 * KB)
        kT = ctx.sb_at("kT", [128, T], BF16, 128 * KB)
        vtm3 = ctx.sb_at("vtm3", [128, NT, DK], BF16, 132 * KB)
        fkT = ctx.sb_at("fkT", [128, NC, T], BF16, 128 * KB)
        fvtm = ctx.sb_at("fvtm", [128, NT, D], BF16, 160 * KB)
        wslab = ctx.sb_at("wslab", [128, NC, D], BF16, 192 * KB)
        dump = ctx.sb_at("dump", [128, T], BF16, 208 * KB)[:, :]
        rtmp = ctx.sb_at("rtmp", [128, 2, D], F32, 212 * KB)
        ctx.actdump = rtmp

        ctx.ln_xt = ctx.sb("ln_xt", [128, 2, D], F32)
        ctx.ln_xnb = ctx.sb("ln_xnb", [128, 2, D], BF16)
        ctx.ln_st = ctx.sb("ln_st", [128, 8], F32)
        ctx.ident = ctx.sb("ident", [128, 128], BF16)
        ctx.identf = ctx.sb("identf", [128, 128], F32)
        ctx.onescol = ctx.sb("onescol", [1, 128], BF16)
        ctx.ones_bcol = ctx.sb("ones_bcol", [128, 1], BF16)
        ctx.scr128 = ctx.sb("scr128", [128, 128], F32)
        stats = ctx.sb("stats", [128, 576], F32)
        small = ctx.sb("small", [1, 160], F32)
        smallb = ctx.sb("smallb", [1, 64], BF16)
        colbf = ctx.sb("colbf", [128, 1], BF16)
        ett = ctx.sb("ett", [128, 2, 512], BF16)
        aoT = ctx.sb("aoT", [128, NC, 128], BF16)
        sstage = ctx.sb("sstage", [128, 2, D], BF16)
        ztm = ctx.sb("ztm", [128, 16], F32)
        gtm = ctx.sb("gtm", [128, 16], F32)
        mtm = ctx.sb("mtm", [128, 16], F32)
        mngtm = ctx.sb("mngtm", [128, 16], F32)
        gqtm = ctx.sb("gqtm", [128, 16], F32)
        biash = ctx.sb("biash", [128, 2, 16], F32)
        indt = ctx.sb("indt", [128, 16], F32)
        gb = ctx.sb("gb", [1, 16], F32)
        mnrow = ctx.sb("mnrow", [1, T], BF16)
        mrowf = ctx.sb("mrowf", [1, T], F32)
        wgb = ctx.sb("wgb", [128, NC], BF16)

        p_a = ctx.ps("p_a", [128, T], F32)
        p_b = ctx.ps("p_b", [128, 1024], F32)
        ctx.p_c = ctx.ps("p_c", [128, 512], F32)
        p_c = ctx.p_c
        ctx.p_tb = ctx.ps("p_tb", [128, 512], BF16)
        p_tb = ctx.p_tb

        emit_const_gen(S, ctx)
        S.op("gpsimd", lambda g: g.dma_start(stats[0:1, SLD:SLD + 1],
                                             scale_d.ap()), inc=ld)
        S.op("gpsimd", lambda g: g.dma_start(wgb[:], wgc_d.ap()), inc=ld)
        S.barrier()

        emit_ln(S, ctx, xb_d, xnT)
        S.barrier()

        wqr = wqT_d.ap().rearrange("(c p) d -> c p d", p=128)
        for cc in range(NC):
            S.op("gpsimd", lambda g, cc=cc: g.dma_start(
                wslab[:, cc, :], wqr[cc]), inc=ld)
        wkr = wkT_d.ap().rearrange("(c p) f -> c p f", p=128)
        wvr = wvT_d.ap().rearrange("(c p) f -> c p f", p=128)
        for cc in range(NC):
            S.op("gpsimd", lambda g, cc=cc: g.dma_start(
                wk_sb[:, cc, :], wkr[cc]), inc=ld)
            S.op("gpsimd", lambda g, cc=cc: g.dma_start(
                wv_sb[:, cc, :], wvr[cc]), inc=ld)
        S.barrier()
        emit_bcast_col(S, ctx, stats[:, SC:SC + 1], stats[0:1, SLD:SLD + 1],
                       smallb[:, 0:1])
        S.barrier()
        emit_proj_T(S, ctx, wslab, xnT, qpT, scale=stats[:, SC:SC + 1])
        for ft in range(T // 512):
            pairs = [(wk_sb[:, cc, :], xnT[:, cc, 512 * ft:512 * (ft + 1)])
                     for cc in range(NC)]
            v = mm_chain(S, ctx, "p_c", p_c[0:DK, 0:512], pairs)
            e = S.op("scalar", lambda sc, ft=ft: sc.activation(
                kT[0:DK, 512 * ft:512 * (ft + 1)], p_c[0:DK, 0:512],
                AF.Identity), wait=[(pe, v)], inc=act)
            ctx.set_guard("p_c", act, e)
        for tb in range(NT):
            pairs = [(xnT[:, cc, 128 * tb:128 * (tb + 1)], wv_sb[:, cc, :])
                     for cc in range(NC)]
            v = mm_chain(S, ctx, "p_c", p_c[:, 0:DK], pairs)
            e = S.op("scalar", lambda sc, tb=tb: sc.activation(
                vtm3[:, tb, :], p_c[:, 0:DK], AF.Identity),
                wait=[(pe, v)], inc=act)
            ctx.set_guard("p_c", act, e)
        S.barrier()
        S.op("sync", lambda g: g.dma_start(kT[DK:128, :], kT[0:DK, :]),
             inc=st_c)
        S.barrier()

        if DBG:
            S.op("sync", lambda g: g.dma_start(
                dbg["d_qpT"].ap(), qpT.ap().rearrange("p a b -> p (a b)")),
                inc=st_c)
            S.op("sync", lambda g: g.dma_start(dbg["d_kT"].ap(), kT.ap()),
                 inc=st_c)
            S.op("sync", lambda g: g.dma_start(
                dbg["d_vtm"].ap(), vtm3.ap().rearrange("p a b -> p (a b)")),
                inc=st_c)
            S.barrier()

        for tb in range(NT):
            pairs = [(xnT[:, cc, 128 * tb:128 * (tb + 1)], wgb[:, cc:cc + 1])
                     for cc in range(NC)]
            v = mm_chain(S, ctx, "p_c", p_c[:, 0:1], pairs)
            e = S.op("vector", lambda vv, tb=tb: vv.tensor_copy(
                ztm[:, tb:tb + 1], p_c[:, 0:1]), wait=[(pe, v)], inc=ev)
            ctx.set_guard("p_c", ev, e)
        e1 = chain(S, "vector", [
            lambda v: v.tensor_copy(ctx.scr128[:, 0:16], ztm[:]),
            lambda v: v.tensor_scalar(ctx.scr128[:, 16:32], ztm[:], -1.0,
                                      None, ALU.mult),
            lambda v: v.reduce_max(stats[:, NM:NM + 1], ctx.scr128[:, 0:32],
                                   axis=AX.X),
            lambda v: v.tensor_copy(colbf[:], stats[:, NM:NM + 1]),
        ], wait=[(ev, ev.n)])
        pv = S.op("tensor", lambda t: t.transpose(
            p_tb[0:1, 0:128], colbf[:], ctx.ident[:]),
            wait=[(ev, e1)] + ctx.guard("p_tb"), inc=pe)
        e2 = chain(S, "vector", [
            lambda v: v.tensor_copy(small[:, 0:128], p_tb[0:1, 0:128]),
            lambda v: v.reduce_max(small[:, 128:129], small[:, 0:128],
                                   axis=AX.X),
            lambda v: v.tensor_scalar(gb[:, 0:1], small[:, 128:129], -1.0,
                                      -1.0, ALU.mult, ALU.add),
            lambda v: v.tensor_scalar(gb[:, 1:2], small[:, 128:129], 1.0,
                                      None, ALU.add),
        ], wait=[(pe, pv)])
        ctx.set_guard("p_tb", ev, e2)
        ga = S.op("scalar", lambda sc: sc.activation(
            gtm[:], ztm[:], AF.Sigmoid), wait=[(ev, e2)], inc=act)
        prev = e2
        for it in range(48):
            c1 = chain(S, "vector", [
                lambda v: v.tensor_tensor(gb[:, 2:3], gb[:, 0:1], gb[:, 1:2],
                                          ALU.add),
                lambda v: v.tensor_scalar(gb[:, 2:3], gb[:, 2:3], 0.5, None,
                                          ALU.mult),
            ], wait=[(ev, prev)])
            mb = emit_bcast_col(S, ctx, stats[:, NM:NM + 1], gb[0:1, 2:3],
                                smallb[:, 2:3], wait=[(ev, c1)])
            iv = S.op("vector", lambda v: v.tensor_scalar(
                indt[:], ztm[:], stats[:, NM:NM + 1], None, ALU.is_gt),
                wait=[(ev, mb)], inc=ev)
            av = S.op("scalar", lambda sc: sc.activation(
                ctx.scr128[:, 32:48], indt[:], AF.Identity,
                accum_out=stats[:, NM:NM + 1]), wait=[(ev, iv)], inc=act)
            cb = S.op("vector", lambda v: v.tensor_copy(
                colbf[:], stats[:, NM:NM + 1]), wait=[(act, av)], inc=ev)
            pv2 = S.op("tensor", lambda t: t.matmul(
                p_c[0:1, 0:1], colbf[:], ctx.ones_bcol[:], start=True,
                stop=True), wait=[(ev, cb)] + ctx.guard("p_c"), inc=pe)
            cpv = S.op("vector", lambda v: v.tensor_copy(
                gb[:, 3:4], p_c[0:1, 0:1]), wait=[(pe, pv2)], inc=ev)
            ctx.set_guard("p_c", ev, cpv)
            prev = chain(S, "vector", [
                lambda v: v.tensor_scalar(gb[:, 4:5], gb[:, 3:4],
                                          CCAP - 0.5, None, ALU.is_gt),
                lambda v: v.tensor_tensor(gb[:, 5:6], gb[:, 2:3], gb[:, 0:1],
                                          ALU.subtract),
                lambda v: v.tensor_tensor(gb[:, 5:6], gb[:, 5:6], gb[:, 4:5],
                                          ALU.mult),
                lambda v: v.tensor_tensor(gb[:, 0:1], gb[:, 0:1], gb[:, 5:6],
                                          ALU.add),
                lambda v: v.tensor_tensor(gb[:, 6:7], gb[:, 1:2], gb[:, 2:3],
                                          ALU.subtract),
                lambda v: v.tensor_tensor(gb[:, 6:7], gb[:, 6:7], gb[:, 4:5],
                                          ALU.mult),
                lambda v: v.tensor_tensor(gb[:, 1:2], gb[:, 6:7], gb[:, 2:3],
                                          ALU.add),
            ], wait=[(ev, cpv)])
        tb_ = emit_bcast_col(S, ctx, stats[:, CH:CH + 1], gb[0:1, 0:1],
                             smallb[:, 3:4], wait=[(ev, prev)])
        g1 = chain(S, "vector", [
            lambda v: v.tensor_scalar(mtm[:], ztm[:], stats[:, CH:CH + 1],
                                      None, ALU.is_gt),
            lambda v: v.tensor_scalar(mngtm[:], mtm[:], 16384.0, -16384.0,
                                      ALU.mult, ALU.add),
            lambda v: v.tensor_tensor(gqtm[:], mtm[:], gtm[:], ALU.mult),
        ], wait=[(ev, tb_), (act, ga)])
        mrv = None
        for tb in range(NT):
            pv3 = S.op("tensor", lambda t, tb=tb: t.transpose(
                p_c[0:1, 0:128], mngtm[:, tb:tb + 1], ctx.identf[:]),
                wait=[(ev, g1)] + ctx.guard("p_c"), inc=pe)
            mrv = S.op("vector", lambda v, tb=tb: v.tensor_copy(
                mrowf[0:1, 128 * tb:128 * (tb + 1)], p_c[0:1, 0:128]),
                wait=[(pe, pv3)], inc=ev)
            ctx.set_guard("p_c", ev, mrv)
        S.op("vector", lambda v: v.tensor_copy(mnrow[:], mrowf[:]),
             wait=[(ev, mrv)], inc=ev)
        S.barrier()

        if DBG:
            dg = chain(S, "vector", [
                lambda v: v.tensor_copy(ctx.scr128[:, 0:16], ztm[:]),
                lambda v: v.tensor_copy(ctx.scr128[:, 16:32], gtm[:]),
                lambda v: v.tensor_copy(ctx.scr128[:, 32:48], mtm[:]),
                lambda v: v.tensor_copy(ctx.scr128[:, 48:64], gqtm[:]),
            ])
            S.op("sync", lambda g: g.dma_start(
                dbg["d_gate"].ap(), ctx.scr128[:, 0:64]), wait=[(ev, dg)],
                inc=st_c)
            S.barrier()

        def head_tail(h, a_last):
            e1 = chain(S, "vector", [
                lambda v, h=h: v.reduce_max(
                    stats[:, CH:CH + 1], stats[:, M0 + 16 * h:M0 + 16 * h + 16],
                    axis=AX.X),
                lambda v: v.tensor_copy(colbf[:], stats[:, CH:CH + 1]),
            ], wait=[(act, a_last)])
            pv = S.op("tensor", lambda t: t.transpose(
                p_tb[0:1, 0:128], colbf[:], ctx.ident[:]),
                wait=[(ev, e1)] + ctx.guard("p_tb"), inc=pe)
            e2 = chain(S, "vector", [
                lambda v: v.tensor_copy(small[:, 0:128], p_tb[0:1, 0:128]),
                lambda v: v.reduce_max(small[:, 128:129], small[:, 0:128],
                                       axis=AX.X),
            ], wait=[(pe, pv)])
            ctx.set_guard("p_tb", ev, e2)
            emit_bcast_col(S, ctx, stats[:, CH:CH + 1], small[:, 128:129],
                           smallb[:, 1:2], wait=[(ev, e2)])
            nv = S.op("vector", lambda v, h=h: v.tensor_scalar(
                stats[:, NC0 + h:NC0 + h + 1], stats[:, CH:CH + 1], -1.0,
                None, ALU.mult), wait=[(ev, ev.n)], inc=ev)
            fa = S.op("scalar", lambda sc, h=h: sc.activation(
                stats[:, F0 + 16 * h:F0 + 16 * h + 16],
                stats[:, M0 + 16 * h:M0 + 16 * h + 16], AF.Exp, scale=-1.0,
                bias=stats[:, CH:CH + 1]), wait=[(ev, nv)], inc=act)
            chain(S, "vector", [
                lambda v: v.reciprocal(stats[:, LI:LI + 16],
                                       stats[:, LQ:LQ + 16]),
                lambda v, h=h: v.tensor_tensor(
                    stats[:, F0 + 16 * h:F0 + 16 * h + 16],
                    stats[:, F0 + 16 * h:F0 + 16 * h + 16],
                    stats[:, LI:LI + 16], ALU.mult),
            ], wait=[(act, fa)])

        def stats_pass(qsrc, ksrc, masked):
            for h in range(NH):
                p0, hc = DK * (h % 2), h // 2
                a_last = None
                for qb in range(NT):
                    v = None
                    for fb in range(T // 512):
                        f0, f1 = 512 * fb, 512 * (fb + 1)
                        pairs = [(qsrc[p0:p0 + DK, hc,
                                       128 * qb:128 * (qb + 1)],
                                  (ksrc[p0:p0 + DK, f0:f1] if not masked
                                   else ksrc[p0:p0 + DK, hc, f0:f1]))]
                        if masked:
                            pairs.append((ctx.onescol[:], mnrow[0:1, f0:f1]))
                        v = mm_chain(S, ctx, "p_a" if fb == 0 else "p_a_x",
                                     p_a[:, f0:f1], pairs)
                    e = chain(S, "vector", [
                        lambda vv, h=h, qb=qb: vv.reduce_max(
                            stats[:, M0 + 16 * h + qb:M0 + 16 * h + qb + 1],
                            p_a[:, 0:T], axis=AX.X),
                        lambda vv, h=h, qb=qb: vv.tensor_scalar(
                            stats[:, NM:NM + 1],
                            stats[:, M0 + 16 * h + qb:M0 + 16 * h + qb + 1],
                            -1.0, None, ALU.mult),
                    ], wait=[(pe, v)])
                    a_last = S.op("scalar", lambda sc, qb=qb: sc.activation(
                        dump[:, 0:T], p_a[:, 0:T], AF.Exp,
                        bias=stats[:, NM:NM + 1],
                        accum_out=stats[:, LQ + qb:LQ + qb + 1]),
                        wait=[(ev, e)], inc=act)
                    ctx.set_guard("p_a", act, a_last)
                head_tail(h, a_last)

        def pv_pass(qsrc, ksrc, vsrc, masked, emit_out):
            ett_war = [0, 0]
            ao_war = []
            bias_war = [[], []]
            for qc in range(4):
                ep = None
                for h in range(NH):
                    p0, hc = DK * (h % 2), h // 2
                    hb = h % 2
                    if masked:
                        bw = S.op("vector", lambda v, h=h, hb=hb:
                                  v.tensor_scalar(
                                      biash[:, hb, :], mngtm[:],
                                      stats[:, NC0 + h:NC0 + h + 1], None,
                                      ALU.add), wait=bias_war[hb], inc=ev)
                        bias_war[hb] = []
                    pv_last = None
                    for kb in range(NT):
                        bi = kb % 2
                        v = S.op("tensor", lambda t, p0=p0, hc=hc, kb=kb,
                                 qc=qc: t.matmul(
                                     p_c[:, 0:512],
                                     (ksrc[p0:p0 + DK,
                                           128 * kb:128 * (kb + 1)]
                                      if not masked else
                                      ksrc[p0:p0 + DK, hc,
                                           128 * kb:128 * (kb + 1)]),
                                     qsrc[p0:p0 + DK, hc,
                                          512 * qc:512 * (qc + 1)],
                                     start=True, stop=True),
                                 wait=ctx.guard("p_c"), inc=pe)
                        bias_ap = (stats[:, NC0 + h:NC0 + h + 1] if not masked
                                   else biash[:, hb, kb:kb + 1])
                        aw = [(pe, v), (pe, ett_war[bi])]
                        if masked:
                            aw.append((ev, bw))
                        a = S.op("scalar", lambda sc, bi=bi, bias_ap=bias_ap:
                                 sc.activation(ett[:, bi, :], p_c[:, 0:512],
                                               AF.Exp, bias=bias_ap),
                                 wait=aw, inc=act)
                        ctx.set_guard("p_c", act, a)
                        for j in range(4):
                            w = []
                            if j == 0:
                                w = [(act, a)]
                                if kb == 0:
                                    w += ctx.guard("p_b")
                            pv_last = S.op("tensor", lambda t, bi=bi, kb=kb,
                                           j=j, h=h: t.matmul(
                                               p_b[:, 64 * j:64 * (j + 1)],
                                               ett[:, bi,
                                                   128 * j:128 * (j + 1)],
                                               (vsrc[:, kb, :] if not masked
                                                else vsrc[:, kb,
                                                          DK * h:DK * (h + 1)]),
                                               start=(kb == 0),
                                               stop=(kb == NT - 1)),
                                           wait=w, inc=pe)
                        ett_war[bi] = pv_last
                        if masked:
                            bias_war[hb].append((act, a))
                    for j in range(4):
                        qb = 4 * qc + j
                        w = []
                        if j == 0:
                            w = [(pe, pv_last)] + ao_war
                        ep = S.op("scalar", lambda sc, j=j, h=h, qb=qb:
                                  sc.activation(
                                      ao4[:, j, DK * h:DK * (h + 1)],
                                      p_b[:, 64 * j:64 * (j + 1)],
                                      AF.Identity,
                                      scale=stats[:, F0 + 16 * h + qb:
                                                  F0 + 16 * h + qb + 1]),
                                  wait=w, inc=act)
                    ctx.set_guard("p_b", act, ep)
                    ao_war = []
                cp = None
                for j in range(4):
                    qb = 4 * qc + j
                    for cc in range(NC):
                        w = ctx.guard("p_tb")
                        if j == 0 and cc == 0:
                            w = w + [(act, ep)]
                        tv = S.op("tensor", lambda t, j=j, cc=cc: t.transpose(
                            p_tb[:, 0:128], ao4[:, j, 128 * cc:128 * (cc + 1)],
                            ctx.ident[:]), wait=w, inc=pe)
                        cp = S.op("vector", lambda v, cc=cc: v.tensor_copy(
                            aoT[:, cc, :], p_tb[:, 0:128]), wait=[(pe, tv)],
                            inc=ev)
                        ctx.set_guard("p_tb", ev, cp)
                    v = None
                    for fb in range(2):
                        pairs = [(aoT[:, cc, :],
                                  wslab[:, cc, 512 * fb:512 * (fb + 1)])
                                 for cc in range(NC)]
                        v = mm_chain(S, ctx, "p_b",
                                     p_b[:, 512 * fb:512 * (fb + 1)], pairs,
                                     wait=([(ev, cp)] if fb == 0 else ()))
                    emit_out(qb, v)
                ao_war = [(pe, pe.n)]

        stats_pass(qpT, kT, False)
        S.barrier()
        if DBG:
            S.op("sync", lambda g: g.dma_start(dbg["d_stats"].ap(),
                                               stats.ap()), inc=st_c)
            S.barrier()
        wor = woT_d.ap().rearrange("(c p) d -> c p d", p=128)
        for cc in range(NC):
            S.op("gpsimd", lambda g, cc=cc: g.dma_start(
                wslab[:, cc, :], wor[cc]), inc=ld)
        S.barrier()

        sst_war = [0, 0]
        sr = s_scr.ap().rearrange("(n p) d -> n p d", p=128)

        def attn_out(qb, v):
            si = qb % 2
            e = S.op("scalar", lambda sc, qb=qb, si=si: sc.activation(
                sstage[:, si, :], p_b[:, 0:1024], AF.Identity),
                wait=[(pe, v), (st_c, sst_war[si])], inc=act)
            ctx.set_guard("p_b", act, e)
            sst_war[si] = S.op("sync", lambda g, qb=qb, si=si: g.dma_start(
                sr[qb], sstage[:, si, :]), wait=[(act, e)], inc=st_c)

        pv_pass(qpT, kT, vtm3, False, attn_out)
        S.barrier()
        if DBG:
            S.op("sync", lambda g: g.dma_start(dbg["d_s"].ap(), s_scr.ap()),
                 inc=st_c)
            S.barrier()

        wqfr = wqf_d.ap().rearrange("(c p) d -> c p d", p=128)
        for cc in range(NC):
            S.op("gpsimd", lambda g, cc=cc: g.dma_start(
                wslab[:, cc, :], wqfr[cc]), inc=ld)
        S.barrier()
        emit_proj_T(S, ctx, wslab, xnT, fqpT, scale=1.0 / 8.0)
        S.barrier()
        wkfr = wkf_d.ap().rearrange("(c p) d -> c p d", p=128)
        for cc in range(NC):
            S.op("gpsimd", lambda g, cc=cc: g.dma_start(
                wslab[:, cc, :], wkfr[cc]), inc=ld)
        S.barrier()
        emit_proj_T(S, ctx, wslab, xnT, fkT)
        S.barrier()
        wvfr = wvf_d.ap().rearrange("(c p) d -> c p d", p=128)
        for cc in range(NC):
            S.op("gpsimd", lambda g, cc=cc: g.dma_start(
                wslab[:, cc, :], wvfr[cc]), inc=ld)
        S.barrier()
        for tb in range(NT):
            for fb in range(2):
                pairs = [(xnT[:, cc, 128 * tb:128 * (tb + 1)],
                          wslab[:, cc, 512 * fb:512 * (fb + 1)])
                         for cc in range(NC)]
                v = mm_chain(S, ctx, "p_c", p_c[:, 0:512], pairs)
                e = S.op("scalar", lambda sc, tb=tb, fb=fb: sc.activation(
                    fvtm[:, tb, 512 * fb:512 * (fb + 1)], p_c[:, 0:512],
                    AF.Identity), wait=[(pe, v)], inc=act)
                ctx.set_guard("p_c", act, e)
        S.barrier()
        if DBG:
            S.op("sync", lambda g: g.dma_start(
                dbg["d_fq"].ap(), fqpT.ap().rearrange("p a b -> p (a b)")),
                inc=st_c)
            S.op("sync", lambda g: g.dma_start(
                dbg["d_fk"].ap(), fkT.ap().rearrange("p a b -> p (a b)")),
                inc=st_c)
            S.op("sync", lambda g: g.dma_start(
                dbg["d_fv"].ap(), fvtm.ap().rearrange("p a b -> p (a b)")),
                inc=st_c)
            S.barrier()

        stats_pass(fqpT, fkT, True)
        S.barrier()
        if DBG:
            S.op("sync", lambda g: g.dma_start(dbg["d_stats2"].ap(),
                                               stats.ap()), inc=st_c)
            S.barrier()
        wofr = wof_d.ap().rearrange("(c p) d -> c p d", p=128)
        for cc in range(NC):
            S.op("gpsimd", lambda g, cc=cc: g.dma_start(
                wslab[:, cc, :], wofr[cc]), inc=ld)
        S.barrier()

        xr = xb_d.ap().rearrange("(n p) d -> n p d", p=128)
        x4r = x4_d.ap().rearrange("(n p) d -> n p d", p=128)
        res_war = [0, 0]

        def ffn_out(qb, v):
            bi2 = qb % 2
            rw_old = res_war[bi2]
            lvx = S.op("gpsimd", lambda g, qb=qb, bi2=bi2: g.dma_start(
                ctx.ln_xt[:, bi2, :], xr[qb]),
                wait=[(st_c, rw_old)], inc=ld)
            lvs = S.op("gpsimd", lambda g, qb=qb, bi2=bi2: g.dma_start(
                ctx.ln_xnb[:, bi2, :], sr[qb]), wait=[(st_c, rw_old)], inc=ld)
            a = S.op("scalar", lambda sc, qb=qb, bi2=bi2: sc.activation(
                rtmp[:, bi2, :], p_b[:, 0:1024], AF.Silu,
                scale=gqtm[:, qb:qb + 1]),
                wait=[(pe, v), (st_c, rw_old)], inc=act)
            ctx.set_guard("p_b", act, a)
            e = chain(S, "vector", [
                lambda vv, bi2=bi2: vv.tensor_tensor(
                    rtmp[:, bi2, :], rtmp[:, bi2, :], ctx.ln_xnb[:, bi2, :],
                    ALU.mult),
                lambda vv, bi2=bi2: vv.tensor_tensor(
                    rtmp[:, bi2, :], rtmp[:, bi2, :], ctx.ln_xt[:, bi2, :],
                    ALU.add),
            ], wait=[(act, a), (ld, lvx), (ld, lvs)])
            res_war[bi2] = S.op("sync", lambda g, qb=qb, bi2=bi2: g.dma_start(
                x4r[qb], rtmp[:, bi2, :]), wait=[(ev, e)], inc=st_c)

        pv_pass(fqpT, fkT, fvtm, True, ffn_out)
        S.barrier()

        block = es.enter_context(nc.Block())
        S.emit(block)
    return nc



_CACHE = {}


def _prep_L0(inputs):
    x = np.ascontiguousarray(np.asarray(inputs["L0_attn_x"], np.float32))
    xf = x.reshape(B * T, D)
    xTf = np.ascontiguousarray(xf.T)
    wqk = np.ascontiguousarray(np.asarray(inputs["L0_attn_W_qk"], np.float32))
    wvT = np.ascontiguousarray(np.asarray(inputs["L0_attn_W_v"],
                                          np.float32).T)
    slope = np.asarray(inputs["L0_attn_alibi_slope"],
                       np.float32).reshape(1, 1)
    wg_r = np.ascontiguousarray(
        np.asarray(inputs["L0_ffn_Wg"], np.float32).reshape(NC, 128, 8)
        .transpose(1, 0, 2).reshape(128, NC * 8))
    w1 = np.ascontiguousarray(
        np.asarray(inputs["L0_ffn_W1"], np.float32).reshape(8 * D, 4096))
    b1 = np.ascontiguousarray(
        np.asarray(inputs["L0_ffn_b1"], np.float32).reshape(8, 32, 128)
        .transpose(0, 2, 1).reshape(8, 32 * 128))
    w2 = np.ascontiguousarray(
        np.asarray(inputs["L0_ffn_W2"], np.float32).reshape(8 * 4096, D))
    b2 = np.ascontiguousarray(np.asarray(inputs["L0_ffn_b2"], np.float32))
    in_maps = []
    for c in range(B):
        sel = np.zeros((128, 16), np.float32)
        for r in range(16):
            sel[16 * c + r, r] = 1.0
        in_maps.append(dict(xb=np.ascontiguousarray(x[c]), xTf=xTf, wqk=wqk,
                            wvT=wvT, slope=slope, wg=wg_r, w1=w1, b1=b1,
                            w2=w2, b2=b2, sel=sel))
    return in_maps


def run_L0(inputs):
    if "L0" not in _CACHE:
        _CACHE["L0"] = build_L0()
    in_maps = _prep_L0(inputs)
    res = run_bass_kernel_spmd(_CACHE["L0"], in_maps, core_ids=list(range(B)))
    return np.stack([np.asarray(res.results[c]["x1"], np.float32)
                     for c in range(B)])


def _prep_L1(inputs, x1):
    dk = np.asarray(inputs["L1_attn_diff_kernel"], np.float32)
    mk = np.asarray(inputs["L1_attn_ma_r_kernel"], np.float32)
    kpad = np.zeros((128, 1024), np.float32)
    for a in range(16):
        for b_ in range(8):
            r = 8 * a + b_
            if b_ < 4:
                kpad[r, 127:127 + 4] = dk[4 * a + b_]
            else:
                kpad[r, 127:127 + 720] = mk[4 * a + (b_ - 4)]
    wg = np.ascontiguousarray(np.asarray(inputs["L1_ffn_w_gate"], np.float32))
    wu = np.ascontiguousarray(np.asarray(inputs["L1_ffn_w_up"], np.float32))
    wd = np.ascontiguousarray(np.asarray(inputs["L1_ffn_w_down"], np.float32))
    return [dict(xb=np.ascontiguousarray(x1[c]), kpad=kpad, wgate=wg,
                 wup=wu, wdown=wd) for c in range(B)]


def run_L1(inputs, x1):
    if "L1" not in _CACHE:
        _CACHE["L1"] = build_L1()
    in_maps = _prep_L1(inputs, x1)
    res = run_bass_kernel_spmd(_CACHE["L1"], in_maps, core_ids=list(range(B)))
    return np.stack([np.asarray(res.results[c]["x2"], np.float32)
                     for c in range(B)])


def _prep_L2(inputs, x2):
    wlh = np.ascontiguousarray(np.asarray(inputs["L2_attn_W_lh"], np.float32))
    wrh = np.ascontiguousarray(np.asarray(inputs["L2_attn_W_rh"], np.float32))
    wq = np.ascontiguousarray(np.asarray(inputs["L2_attn_Wq"], np.float32))
    wvT = np.ascontiguousarray(np.asarray(inputs["L2_attn_Wv"], np.float32).T)
    wcT = np.ascontiguousarray(np.asarray(inputs["L2_attn_Wc"], np.float32).T)
    idv = np.ascontiguousarray(
        np.asarray(inputs["L2_attn_identity"], np.float32).reshape(8, 128).T)
    w1 = np.ascontiguousarray(np.asarray(inputs["L2_ffn_W1"], np.float32))
    b1 = np.ascontiguousarray(
        np.asarray(inputs["L2_ffn_b1"], np.float32).reshape(32, 128).T
        .reshape(1, 32 * 128))
    w2 = np.ascontiguousarray(np.asarray(inputs["L2_ffn_W2"], np.float32))
    b2 = np.ascontiguousarray(
        np.asarray(inputs["L2_ffn_b2"], np.float32).reshape(1, D))
    com = dict(wlh=wlh, wlhT=np.ascontiguousarray(wlh.T), wrh=wrh,
               wrhT=np.ascontiguousarray(wrh.T), wq=wq, wvT=wvT, wcT=wcT,
               idv=idv, w1=w1, b1=b1, w2=w2, b2=b2)
    return [dict(xb=np.ascontiguousarray(x2[c]), **com) for c in range(B)]


def run_L2(inputs, x2):
    if "L2" not in _CACHE:
        _CACHE["L2"] = build_L2()
    in_maps = _prep_L2(inputs, x2)
    res = run_bass_kernel_spmd(_CACHE["L2"], in_maps, core_ids=list(range(B)))
    return np.stack([np.asarray(res.results[c]["x3"], np.float32)
                     for c in range(B)])


def _prep_L3(inputs, x3):
    wqT = np.ascontiguousarray(np.asarray(inputs["L3_attn_w_q"],
                                          np.float32).T)
    wkT = np.ascontiguousarray(np.asarray(inputs["L3_attn_w_k"],
                                          np.float32).T)
    wvT = np.ascontiguousarray(np.asarray(inputs["L3_attn_w_v"],
                                          np.float32).T)
    woT = np.ascontiguousarray(np.asarray(inputs["L3_attn_w_out"],
                                          np.float32).T)
    scale = np.asarray(inputs["L3_attn_softmax_scale"],
                       np.float32).reshape(1, 1)
    wgc = np.ascontiguousarray(
        np.asarray(inputs["L3_ffn_Wg"], np.float32).reshape(NC, 128).T)
    wqf = np.ascontiguousarray(np.asarray(inputs["L3_ffn_Wq"], np.float32))
    wkf = np.ascontiguousarray(np.asarray(inputs["L3_ffn_Wk"], np.float32))
    wvf = np.ascontiguousarray(np.asarray(inputs["L3_ffn_Wv"], np.float32))
    wof = np.ascontiguousarray(np.asarray(inputs["L3_ffn_Wo"], np.float32))
    com = dict(wqT=wqT, wkT=wkT, wvT=wvT, woT=woT, scale=scale, wgc=wgc,
               wqf=wqf, wkf=wkf, wvf=wvf, wof=wof)
    return [dict(xb=np.ascontiguousarray(x3[c]), **com) for c in range(B)]


def run_L3(inputs, x3):
    if "L3" not in _CACHE:
        _CACHE["L3"] = build_L3()
    in_maps = _prep_L3(inputs, x3)
    res = run_bass_kernel_spmd(_CACHE["L3"], in_maps, core_ids=list(range(B)))
    return np.stack([np.asarray(res.results[c]["x4"], np.float32)
                     for c in range(B)])


def kernel(**inputs):
    x = run_L0(inputs)
    x = run_L1(inputs, x)
    x = run_L2(inputs, x)
    x = run_L3(inputs, x)
    return x
